# version 26
# baseline (speedup 1.0000x reference)
import contextlib
import numpy as np
import concourse.bass as bass
import concourse.mybir as mybir
from concourse.bass_utils import run_bass_kernel_spmd

F32 = mybir.dt.float32
BF16 = mybir.dt.bfloat16
I32 = mybir.dt.int32
U8 = mybir.dt.uint8
AF = mybir.ActivationFunctionType
ALU = mybir.AluOpType
AX = mybir.AxisListType

NCORES = 8
SET_LN = 6
SET_SIG = 2
SET_GELU = 11
LNP = 16


def load_act_set(nc, eng, idx):
    eng.add_instruction(mybir.InstLoadActFuncSet(
        name=nc.get_next_instruction_name(), act_func_set_id=idx, ins=[], outs=[]))


class Cfg:
    def __init__(self, T, C, N):
        assert T % 128 == 0 and C % 256 == 0 and N % 1024 == 0
        self.T, self.C, self.N = T, C, N
        self.D = 2 * C
        self.TT = T // 128
        self.CT = C // 128
        self.NT = N // 128
        self.TCW = min(T, 512)
        self.TC = T // self.TCW
        self.JU = C // 2
        self.BU = C // 32
        self.Ds = self.D // NCORES
        self.Ns = self.N // NCORES
        self.Cs = self.C // NCORES
        self.NG7 = N // 512
        self.HH = C // 2
        self.HCW = min(self.HH, 512)
        self.HC = self.HH // self.HCW
        self.G8 = min(N, 1024)
        self.NG8 = N // self.G8
        self.NT8 = self.G8 // 128


def build_program(cfg, stop_after=None):
    T, C, D, N = cfg.T, cfg.C, cfg.D, cfg.N
    TT, CT, NT = cfg.TT, cfg.CT, cfg.NT
    TC, TCW = cfg.TC, cfg.TCW
    HH, HC, HCW = cfg.HH, cfg.HC, cfg.HCW
    JU, BU = cfg.JU, cfg.BU
    nc = bass.Bass("TRN2", target_bir_lowering=False, detect_race_conditions=False)

    x_in = nc.dram_tensor("x", [T, C], F32, kind="ExternalInput")
    W_slT = nc.dram_tensor("W_slT", [C, cfg.Ds], F32, kind="ExternalInput")
    b_in = nc.dram_tensor("b", [1, D], F32, kind="ExternalInput")
    gW_sl = nc.dram_tensor("gW_sl", [cfg.Ns, C // 2], I32, kind="ExternalInput")
    gS_sl = nc.dram_tensor("gS_sl", [cfg.Ns, C // 32], F32, kind="ExternalInput")
    uW_sl = nc.dram_tensor("uW_sl", [cfg.Ns, C // 2], I32, kind="ExternalInput")
    uS_sl = nc.dram_tensor("uS_sl", [cfg.Ns, C // 32], F32, kind="ExternalInput")
    dW_sl = nc.dram_tensor("dW_sl", [cfg.Cs, N // 2], I32, kind="ExternalInput")
    dS_sl = nc.dram_tensor("dS_sl", [cfg.Cs, N // 32], F32, kind="ExternalInput")
    out = nc.dram_tensor("out", [T, C], F32, kind="ExternalOutput")

    Wbf_in = nc.dram_tensor("Wbf_in", [C, cfg.Ds], BF16)
    Wbf = nc.dram_tensor("Wbf", [NCORES * C, cfg.Ds], BF16)
    gbf_in = nc.dram_tensor("gbf_in", [cfg.Ns, C], BF16)
    NCH = max(1, (N // NCORES) // 512)
    RPC = (N // NCORES) // NCH
    gbfc = [nc.dram_tensor(f"gbf{c}", [RPC * NCORES, C], BF16) for c in range(NCH)]
    ubf_in = nc.dram_tensor("ubf_in", [cfg.Ns, C], BF16)
    ubfc = [nc.dram_tensor(f"ubf{c}", [RPC * NCORES, C], BF16) for c in range(NCH)]
    dbf_in = nc.dram_tensor("dbf_in", [cfg.Cs, N], BF16)
    dbf = nc.dram_tensor("dbf", [C, N], BF16)
    ln1d = nc.dram_tensor("ln1d", [T, C], BF16)
    hhi = nc.dram_tensor("hhi", [C, T], BF16)
    hlo = nc.dram_tensor("hlo", [C, T], BF16)
    x2d = nc.dram_tensor("x2d", [T, C], F32)
    Id = nc.dram_tensor("Id", [N, T], BF16)
    dhalf = nc.dram_tensor("dhalf", [T, HH], F32)

    top = contextlib.ExitStack()
    sem = lambda name: top.enter_context(nc.semaphore(name))
    sbT = lambda st, name, shape, dt: st.enter_context(nc.sbuf_tensor(name, shape, dt))
    psT = lambda st, name, shape: st.enter_context(nc.psum_tensor(name, shape, F32))

    with top:
        c_m1 = sbT(top, "c_m1", [128, 1], F32)
        c_75 = sbT(top, "c_75", [128, 1], F32)
        c_eps = sbT(top, "c_eps", [128, 1], F32)
        c_mh = sbT(top, "c_mh", [128, 1], F32)
        c_zero = sbT(top, "c_zero", [128, 1], F32)
        ones_bf = sbT(top, "ones_bf", [1, 512], BF16)
        b_bf = sbT(top, "b_bf", [1, D], BF16)
        NCOL = 4 * TT
        st_sum = sbT(top, "st_sum", [128, NCOL], F32)
        st_sq = sbT(top, "st_sq", [128, NCOL], F32)
        st_m = sbT(top, "st_m", [128, NCOL], F32)
        st_v0 = sbT(top, "st_v0", [128, NCOL], F32)
        st_nv = sbT(top, "st_nv", [128, NCOL], F32)
        st_ln = sbT(top, "st_ln", [128, NCOL], F32)
        st_r = sbT(top, "st_r", [128, NCOL], F32)
        st_nmr = sbT(top, "st_nmr", [128, NCOL], F32)
        idm = sbT(top, "idm", [128, 128], BF16)

        ln_sems = [sem(f"s_ln{i}") for i in range(LNP)]

        def lnsem(col):
            return ln_sems[col % LNP], 8 * (col // LNP)

        def ln_v_stats(v, col, src_ap, F):
            s, base = lnsem(col)
            cs = slice(col, col + 1)
            v.wait_ge(s, base + 2)
            v.tensor_scalar(st_m[:, cs], st_sum[:, cs], 1.0 / F, None, ALU.mult).then_inc(s, 1)
            v.tensor_scalar(st_v0[:, cs], st_sq[:, cs], 1.0 / F, None, ALU.mult).then_inc(s, 1)
            v.wait_ge(s, base + 4)
            v.tensor_scalar(st_nv[:, cs], st_m[:, cs], st_m[:, cs], st_v0[:, cs],
                            ALU.mult, ALU.subtract).then_inc(s, 1)

        def ln_v_nmr(v, col):
            s, base = lnsem(col)
            cs = slice(col, col + 1)
            v.wait_ge(s, base + 7)
            v.scalar_tensor_tensor(st_nmr[:, cs], st_m[:, cs], -1.0, st_r[:, cs],
                                   ALU.mult, ALU.mult).then_inc(s, 1)

        def ln_v_norm(v, col, out_ap, src_ap):
            s, base = lnsem(col)
            cs = slice(col, col + 1)
            v.wait_ge(s, base + 8)
            return v.tensor_scalar(out_ap, src_ap, st_r[:, cs], st_nmr[:, cs],
                                   ALU.mult, ALU.add)

        def ln_a_chain(a, col, src_ap, scr_ap):
            s, base = lnsem(col)
            cs = slice(col, col + 1)
            a.activation(scr_ap, src_ap, AF.Copy, accum_out=st_sum[:, cs]).then_inc(s, 1)
            a.activation(scr_ap, src_ap, AF.Square, accum_out=st_sq[:, cs]).then_inc(s, 1)
            a.wait_ge(s, base + 5)
            a.activation(st_ln[:, cs], st_nv[:, cs], AF.Ln,
                         bias=c_eps[:], scale=c_m1[:]).then_inc(s, 1)
            a.wait_ge(s, base + 6)
            a.activation(st_r[:, cs], st_ln[:, cs], AF.Exp,
                         bias=c_zero[:], scale=c_mh[:]).then_inc(s, 1)

        units = []
        for nt in range(cfg.Ns // 128):
            units.append((gW_sl, gS_sl, gbf_in, nt, 0))
        NU_G = len(units)
        for nt in range(cfg.Ns // 128):
            units.append((uW_sl, uS_sl, ubf_in, nt, 0))
        NU_U = len(units)
        for nt in range(cfg.Cs // 128):
            for jc in range((N // 2) // JU):
                units.append((dW_sl, dS_sl, dbf_in, nt, jc))
        NU = len(units)

        bi = contextlib.ExitStack()
        p8 = [sbT(bi, f"p8_{i}", [128, JU], U8) for i in range(2)]
        scs = sbT(bi, "scs", [128, NU * BU], F32)
        c8 = sbT(bi, "c8", [128, C], U8)
        cf = sbT(bi, "cf", [128, C], BF16)
        sgn = sbT(bi, "sgn", [128, C], BF16)
        dxv = sbT(bi, "dxv", [128, C], BF16)
        dt3 = sbT(bi, "dt3", [128, C], BF16)
        wout = [sbT(bi, f"wout_{i}", [128, C], BF16) for i in range(2)]
        itf = sbT(bi, "itf", [128, 128], I32)
        itp = sbT(bi, "itp", [128, 128], I32)
        xbuf = [sbT(bi, f"xbuf_{i}", [128, C], F32) for i in range(2)]
        lnbuf = [sbT(bi, f"lnbuf_{i}", [128, C], BF16) for i in range(2)]
        wtg = [sbT(bi, f"wtg_{p}", [128, CT * 256], BF16) for p in range(2)]
        wtv = [sbT(bi, f"wtv_{p}", [128, CT * 256], BF16) for p in range(2)]
        gsb = sbT(bi, "gsb", [128, T], F32)
        vsb = sbT(bi, "vsb", [128, T], F32)
        hsb = sbT(bi, "hsb", [128, T], F32)
        hib = sbT(bi, "hib", [128, T], BF16)
        hlb = sbT(bi, "hlb", [128, T], BF16)
        pT = contextlib.ExitStack()
        actT = pT.enter_context(nc.sbuf_tensor("actT", [128, CT * T], BF16, side="right"))
        psA = contextlib.ExitStack()
        PW = max(T, C)
        ps_g = psT(psA, "ps_g", [128, PW])
        ps_v = psT(psA, "ps_v", [128, PW])

        s_prep = sem("s_prep")
        s_ccw = sem("s_ccw")
        s_ccg = [sem(f"s_ccg{c}") for c in range(NCH)]
        s_ccu = [sem(f"s_ccu{c}") for c in range(NCH)]
        s_ccd = sem("s_ccd")
        s_pk = [sem("s_pk0"), sem("s_pk1")]
        s_sc = sem("s_sc")
        s_nib = sem("s_nib")
        s_dec = sem("s_dec")
        s_wo = sem("s_wo")
        s_xl = [sem("s_xl0"), sem("s_xl1")]
        s_n1 = sem("s_n1")
        s_o1 = sem("s_o1")
        s_t1 = sem("s_t1")
        s_wt = [sem("s_wt0"), sem("s_wt1")]
        s_mm = sem("s_mm")
        s_ev = sem("s_ev")
        s_hv = sem("s_hv")
        s_hs = sem("s_hs")
        s_gm = sem("s_gm")
        s_im = sem("s_im")
        RG = [list(range(NCORES))]

        with nc.Block() as blk:
            @blk.gpsimd
            def _(g):
                g.iota(itf[:], [[1, 128]], base=0, channel_multiplier=0).then_inc(s_im, 1)
                g.iota(itp[:], [[0, 128]], base=0, channel_multiplier=1).then_inc(s_im, 1)
                g.memset(c_m1[:], -1.0)
                g.memset(c_75[:], 7.5)
                g.memset(c_eps[:], 1e-5)
                g.memset(c_mh[:], -0.5)
                g.memset(c_zero[:], 0.0)
                g.memset(ones_bf[:], 1.0)
                g.dma_start(Wbf_in[:, :], W_slT[:, :]).then_inc(s_prep, 16)
                g.dma_start(b_bf[:, :], b_in[:, :]).then_inc(s_prep, 16)
                g.wait_ge(s_prep, 16)
                g.collective_compute("AllGather", ALU.bypass, replica_groups=RG,
                                     ins=[Wbf_in.ap().opt()], outs=[Wbf.ap().opt()]
                                     ).then_inc(s_ccw, 1)

                def emit_out(i):
                    _, _, dst, nt, jc = units[i]
                    g.wait_ge(s_dec, i + 1)
                    g.dma_start(dst[nt * 128:(nt + 1) * 128, jc * C:(jc + 1) * C],
                                wout[i % 2][:]).then_inc(s_wo, 16)


                for i, (src, ssrc, dst, nt, jc) in enumerate(units):
                    if i >= 2:
                        g.wait_ge(s_nib, i - 1)
                    g.dma_start(p8[i % 2][:],
                                src[nt * 128:(nt + 1) * 128, jc * JU:(jc + 1) * JU]
                                ).then_inc(s_pk[i % 2], 16)
                    if i >= 1:
                        emit_out(i - 1)
                emit_out(NU - 1)
                g.wait_ge(s_wo, 16 * NU)
                g.wait_ge(s_hs, 32 * CT)
                g.collective_compute("AllGather", ALU.bypass, replica_groups=RG,
                                     ins=[gbf_in[0:RPC, :].opt()],
                                     outs=[gbfc[0].ap().opt()]).then_inc(s_ccg[0], 1)
                g.collective_compute("AllGather", ALU.bypass, replica_groups=RG,
                                     ins=[ubf_in[0:RPC, :].opt()],
                                     outs=[ubfc[0].ap().opt()]).then_inc(s_ccu[0], 1)

            @blk.sync
            def _(s):
                for i in range(TT):
                    if i >= 2:
                        s.wait_ge(s_n1, i - 1)
                    s.dma_start(xbuf[i % 2][:], x_in[i * 128:(i + 1) * 128, :]
                                ).then_inc(s_xl[i % 2], 16)
                for i, (src, ssrc, dst, nt, jc) in enumerate(units):
                    s.dma_start(scs[:, i * BU:(i + 1) * BU],
                                ssrc[nt * 128:(nt + 1) * 128, jc * BU:(jc + 1) * BU]
                                ).then_inc(s_sc, 16)
                s.wait_ge(s_ccw, 1)

                def wt_sp(q):
                    if q >= 2:
                        s.wait_ge(s_mm, 4 * (q - 1))
                    fg = q * 256
                    rg, cg = fg // cfg.Ds, fg % cfg.Ds
                    fv = C + q * 256
                    rv, cv_ = fv // cfg.Ds, fv % cfg.Ds
                    for j in range(CT):
                        s.dma_start(wtg[q % 2][:, j * 256:(j + 1) * 256],
                                    Wbf[rg * C + j * 128:rg * C + (j + 1) * 128,
                                        cg:cg + 256]).then_inc(s_wt[q % 2], 16)
                        s.dma_start(wtv[q % 2][:, j * 256:(j + 1) * 256],
                                    Wbf[rv * C + j * 128:rv * C + (j + 1) * 128,
                                        cv_:cv_ + 256]).then_inc(s_wt[q % 2], 16)

                NQ = CT // 2
                wt_sp(0)
                wt_sp(1)
                s.wait_ge(s_o1, 16 * TT)
                for j in range(CT):
                    s.dma_start(actT[:, j * T:(j + 1) * T],
                                ln1d[:, j * 128:(j + 1) * 128], transpose=True
                                ).then_inc(s_t1, 16)
                for q in range(2, NQ):
                    wt_sp(q)

            @blk.vector
            def _(v):
                v.wait_ge(s_im, 2)
                v.tensor_tensor(idm[:], itf[:], itp[:], ALU.is_equal).then_inc(s_im, 1)

                def a1_tile(i):
                    v.wait_ge(s_xl[i % 2], 16 * (i // 2 + 1))
                    ln_v_stats(v, i, xbuf[i % 2][:, :], C)
                    ln_v_nmr(v, i)
                    if i >= 2:
                        v.wait_ge(s_o1, 16 * i)
                    ln_v_norm(v, i, lnbuf[i % 2][:], xbuf[i % 2][:]).then_inc(s_n1, 1)

                def dec_unit(i):
                    pb = p8[i % 2]
                    v.wait_ge(s_pk[i % 2], 16 * (i // 2 + 1))
                    cv = c8.ap().rearrange("a (j two) -> a j two", two=2)
                    v.tensor_scalar(cv[:, :, 0:1], pb.ap().unsqueeze(2), 15, None,
                                    ALU.bitwise_and)
                    v.tensor_scalar(cv[:, :, 1:2], pb.ap().unsqueeze(2), 4, None,
                                    ALU.logical_shift_right)
                    v.tensor_copy(cf[:], c8[:]).then_inc(s_nib, 1)
                    sg = sgn
                    v.tensor_scalar(sg[:], cf[:], -2.0, 15.0, ALU.mult, ALU.add)
                    v.tensor_scalar(sg[:], sg[:], -1.0, None, ALU.max)
                    v.tensor_scalar(sg[:], sg[:], 1.0, None, ALU.min)
                    v.scalar_tensor_tensor(dxv[:], sg[:], 4.0, cf[:], ALU.mult, ALU.add)
                    v.tensor_scalar(cf[:], dxv[:], 0.5, -2.0, ALU.mult, ALU.add)
                    v.scalar_tensor_tensor(cf[:], dxv[:], 6.0, cf[:], ALU.subtract, ALU.max)
                    v.tensor_scalar(dt3[:], dxv[:], 2.0, -16.0, ALU.mult, ALU.add)
                    v.tensor_tensor(dt3[:], cf[:], dt3[:], ALU.max)
                    v.tensor_tensor(dxv[:], dt3[:], sg[:], ALU.mult)
                    if i >= 2:
                        v.wait_ge(s_wo, 16 * i)
                    if i == 0:
                        v.wait_ge(s_sc, 16 * NU)
                    vv = dxv.ap().rearrange("a (b k) -> a b k", k=32)
                    wv = wout[i % 2].ap().rearrange("a (b k) -> a b k", k=32)
                    scb = scs[:, i * BU:(i + 1) * BU].unsqueeze(2).broadcast_to([128, BU, 32])
                    v.tensor_tensor(wv, vv, scb, ALU.mult).then_inc(s_dec, 1)

                def scan_pair(k):
                    v.wait_ge(s_ev, 2 * (k + 1))
                    if k >= 1:
                        v.wait_ge(s_hs, 32 * k)
                    v.tensor_tensor_scan(hsb[:], gsb[:], vsb[:], 0.0, ALU.mult, ALU.add)
                    v.tensor_copy(hib[:], hsb[:])
                    v.tensor_tensor(hlb[:], hsb[:], hib[:], ALU.subtract).then_inc(s_hv, 1)

                for i in range(TT):
                    a1_tile(i)
                ui = 0

                def take(n):
                    nonlocal ui
                    stop = min(ui + n, NU)
                    while ui < stop:
                        dec_unit(ui)
                        ui += 1

                take(2)
                for k in range(CT):
                    scan_pair(k)
                    take(3)
                take(NU)

            @blk.scalar
            def _(a):
                load_act_set(nc, a, SET_LN)
                for i in range(TT):
                    a.wait_ge(s_xl[i % 2], 16 * (i // 2 + 1))
                    ln_a_chain(a, i, xbuf[i % 2][:, :], ps_g[:, 0:C])
                    a.wait_ge(s_n1, i + 1)
                    nc.scalar.dma_start(ln1d[i * 128:(i + 1) * 128, :], lnbuf[i % 2][:]
                                        ).then_inc(s_o1, 16)
                load_act_set(nc, a, SET_SIG)
                for k in range(CT):
                    if k >= 1:
                        a.wait_ge(s_hv, k)
                        nc.scalar.dma_start(hhi[(k - 1) * 128:k * 128, :], hib[:]
                                            ).then_inc(s_hs, 16)
                        nc.scalar.dma_start(hlo[(k - 1) * 128:k * 128, :], hlb[:]
                                            ).then_inc(s_hs, 16)
                    a.wait_ge(s_mm, 2 * k + 1)
                    if k >= 1:
                        a.wait_ge(s_hv, k)
                    a.activation(gsb[:], ps_g[:, 0:T], AF.Sigmoid).then_inc(s_ev, 1)
                    a.wait_ge(s_mm, 2 * k + 2)
                    a.activation(vsb[:], ps_v[:, 0:T], AF.Tanh).then_inc(s_ev, 1)
                a.wait_ge(s_hv, CT)
                nc.scalar.dma_start(hhi[(CT - 1) * 128:CT * 128, :], hib[:]).then_inc(s_hs, 16)
                nc.scalar.dma_start(hlo[(CT - 1) * 128:CT * 128, :], hlb[:]).then_inc(s_hs, 16)

            @blk.tensor
            def _(t):
                t.wait_ge(s_t1, 16 * CT)
                t.wait_ge(s_prep, 32)
                for k in range(CT):
                    q, kq = divmod(k, 2)
                    if kq == 0:
                        t.wait_ge(s_wt[q % 2], 32 * CT * (q // 2 + 1))
                    for psum, wtiles, boff in ((ps_g, wtg[q % 2], 0), (ps_v, wtv[q % 2], C)):
                        if k >= 1:
                            t.wait_ge(s_ev, 2 * k - (1 if boff == 0 else 0))
                        for j in range(CT):
                            for tc in range(TC):
                                t.matmul(psum[:, tc * TCW:(tc + 1) * TCW],
                                         wtiles[:, j * 256 + kq * 128:
                                                j * 256 + (kq + 1) * 128],
                                         actT[:, j * T + tc * TCW: j * T + (tc + 1) * TCW],
                                         start=(j == 0), stop=False)
                        mm = None
                        for tc in range(TC):
                            mm = t.matmul(psum[:, tc * TCW:(tc + 1) * TCW],
                                          b_bf[0:1, boff + k * 128: boff + (k + 1) * 128],
                                          ones_bf[0:1, 0:TCW], start=False, stop=True)
                        mm.then_inc(s_mm, 1)

        bi.close()
        psA.close()

        a4 = contextlib.ExitStack()
        psB = contextlib.ExitStack()
        ps2 = [psB.enter_context(nc.psum_tensor(f"ps2_{p}", [128, C], BF16))
               for p in range(2)]
        ps_h = psB.enter_context(nc.psum_tensor("ps_h", [128, C], BF16))
        ps_l = psB.enter_context(nc.psum_tensor("ps_l", [128, C], BF16))
        xbuf2 = [sbT(a4, f"xbuf2_{p}", [128, C], F32) for p in range(4)]
        lnbuf2 = [sbT(a4, f"lnbuf2_{p}", [128, C], BF16) for p in range(2)]
        hif = [sbT(a4, f"hif_{p}", [128, C], BF16) for p in range(2)]
        lof = [sbT(a4, f"lof_{p}", [128, C], BF16) for p in range(2)]
        hrow = [sbT(a4, f"hrow_{p}", [128, C], F32) for p in range(2)]
        lsb = sbT(a4, "lsb", [128, C], BF16)
        alnb = sbT(a4, "alnb", [128, C], F32)
        x2b = [sbT(a4, f"x2b_{p}", [128, C], F32) for p in range(2)]
        scr4 = sbT(a4, "scr4", [128, C], F32)
        s_hf = [sem("s_hf0"), sem("s_hf1")]
        s_xl2 = [sem(f"s_xl2{p}") for p in range(4)]
        s_thm = sem("s_thm")
        s_hr = sem("s_hr")
        s_n2 = sem("s_n2")
        s_o2x = [sem("s_o2x0"), sem("s_o2x1")]
        s_tm = sem("s_tm")
        s_tc = sem("s_tc")

        hhi_v = hhi.ap().rearrange("(k p) t -> p k t", p=128)
        hlo_v = hlo.ap().rearrange("(k p) t -> p k t", p=128)

        with nc.Block() as blk:
            @blk.sync
            def _(s):
                s.wait_ge(s_hs, 32 * CT)
                for i in range(TT):
                    if i >= 2:
                        s.wait_ge(s_thm, i - 1)
                    if i >= 4:
                        s.wait_ge(s_n2, 2 * (i - 4) + 1)
                    dst = hif[i % 2].ap().rearrange("p (k t) -> p k t", t=128)
                    s.dma_start(dst, hhi_v[:, :, i * 128:(i + 1) * 128]
                                ).then_inc(s_hf[i % 2], 16)
                    dst = lof[i % 2].ap().rearrange("p (k t) -> p k t", t=128)
                    s.dma_start(dst, hlo_v[:, :, i * 128:(i + 1) * 128]
                                ).then_inc(s_hf[i % 2], 16)
                    s.dma_start(xbuf2[i % 4][:], x_in[i * 128:(i + 1) * 128, :]
                                ).then_inc(s_xl2[i % 4], 16)

            @blk.tensor
            def _(t):
                t.wait_ge(s_im, 3)
                for i in range(TT + 2):
                    if i < TT:
                        t.wait_ge(s_hf[i % 2], 32 * (i // 2 + 1))
                        if i >= 1:
                            t.wait_ge(s_hr, i)
                        mm = None
                        for k in range(CT):
                            mm = t.transpose(ps_h[:, k * 128:(k + 1) * 128],
                                             hif[i % 2][:, k * 128:(k + 1) * 128],
                                             idm[:])
                        for k in range(CT):
                            mm = t.transpose(ps_l[:, k * 128:(k + 1) * 128],
                                             lof[i % 2][:, k * 128:(k + 1) * 128],
                                             idm[:])
                        mm.then_inc(s_thm, 1)
                    if i >= 2:
                        m = i - 2
                        t.wait_ge(s_n2, 2 * m + 2)
                        if m >= 2:
                            t.wait_ge(s_tc, CT * (m - 1))
                        mm = None
                        for j in range(CT):
                            mm = t.transpose(ps2[m % 2][:, j * 128:(j + 1) * 128],
                                             lnbuf2[m % 2][:, j * 128:(j + 1) * 128],
                                             idm[:])
                        mm.then_inc(s_tm, 1)

            @blk.scalar
            def _(a):
                load_act_set(nc, a, SET_LN)
                for i in range(TT):
                    ca, cb_ = TT + 2 * i, TT + 2 * i + 1
                    a.wait_ge(s_hr, i + 1)
                    ln_a_chain(a, ca, hrow[i % 2][:, :], scr4[:])
                    a.wait_ge(s_n2, 2 * i + 1)
                    ln_a_chain(a, cb_, x2b[i % 2][:, :], scr4[:])
                    if i >= 2:
                        a.wait_ge(s_tm, i - 1)
                        m = i - 2
                        for j in range(CT):
                            a.activation(actT[:, j * T + m * 128:j * T + (m + 1) * 128],
                                         ps2[m % 2][:, j * 128:(j + 1) * 128],
                                         AF.Copy).then_inc(s_tc, 1)
                for m in (TT - 2, TT - 1):
                    a.wait_ge(s_tm, m + 1)
                    for j in range(CT):
                        a.activation(actT[:, j * T + m * 128:j * T + (m + 1) * 128],
                                     ps2[m % 2][:, j * 128:(j + 1) * 128],
                                     AF.Copy).then_inc(s_tc, 1)

            @blk.vector
            def _(v):
                for i in range(TT):
                    ca, cb_ = TT + 2 * i, TT + 2 * i + 1
                    v.wait_ge(s_thm, i + 1)
                    v.tensor_copy(lsb[:], ps_l[:])
                    v.tensor_tensor(hrow[i % 2][:], ps_h[:], lsb[:],
                                    ALU.add).then_inc(s_hr, 1)
                    ln_v_stats(v, ca, hrow[i % 2][:, :], C)
                    ln_v_nmr(v, ca)
                    sA, baseA = lnsem(ca)
                    v.wait_ge(sA, baseA + 8)
                    v.tensor_scalar(alnb[:], hrow[i % 2][:], st_r[:, ca:ca + 1],
                                    st_nmr[:, ca:ca + 1], ALU.mult, ALU.add)
                    v.wait_ge(s_xl2[i % 4], 16 * (i // 4 + 1))
                    if i >= 2:
                        v.wait_ge(s_o2x[i % 2], 16 * (i // 2))
                    v.tensor_tensor(x2b[i % 2][:], alnb[:], xbuf2[i % 4][:],
                                    ALU.add).then_inc(s_n2, 1)
                    ln_v_stats(v, cb_, x2b[i % 2][:, :], C)
                    ln_v_nmr(v, cb_)
                    if i >= 2:
                        v.wait_ge(s_tm, i - 1)
                    ln_v_norm(v, cb_, lnbuf2[i % 2][:], x2b[i % 2][:]).then_inc(s_n2, 1)

            @blk.gpsimd
            def _(g):
                for i in range(TT):
                    g.wait_ge(s_n2, 2 * i + 1)
                    g.dma_start(x2d[i * 128:(i + 1) * 128, :], x2b[i % 2][:]).then_inc(s_o2x[i % 2], 16)

        a4.close()
        psB.close()

        if stop_after == "BI":
            return nc

        b7 = contextlib.ExitStack()
        wgt = [[sbT(b7, f"wgt_{p}_{j}", [128, 512], BF16) for j in range(CT)] for p in range(3)]
        wut = [[sbT(b7, f"wut_{p}_{j}", [128, 512], BF16) for j in range(CT)] for p in range(3)]
        glb = [sbT(b7, f"glb_{p}", [128, T], F32) for p in range(2)]
        ib = [sbT(b7, f"ib_{p}", [128, T], BF16) for p in range(2)]
        ps_G = psT(b7, "ps_G", [128, T])
        ps_U = psT(b7, "ps_U", [128, T])
        s_wt2 = [sem("s_wt20"), sem("s_wt21"), sem("s_wt22")]
        s_wu2 = [sem("s_wu20"), sem("s_wu21"), sem("s_wu22")]
        s_ge = sem("s_ge")
        s_iv = sem("s_iv")
        s_io = sem("s_io")
        NG7 = cfg.NG7

        g7order = [(c, r) for c in range(NCH) for r in range(NCORES)]
        assert len(g7order) * 512 == N
        id_base = [r * cfg.Ns + c * RPC for (c, r) in g7order]

        with nc.Block() as blk:
            @blk.sync
            def _(s):
                for g7, (c, r) in enumerate(g7order):
                    if r == 0:
                        s.wait_ge(s_ccg[c], 1)
                        s.wait_ge(s_ccu[c], 1)
                    if g7 >= 3:
                        s.wait_ge(s_gm, 8 * TC * (g7 - 2))
                    r0 = r * RPC
                    for j in range(CT):
                        s.dma_start(wgt[g7 % 3][j][:],
                                    gbfc[c][r0:r0 + 512, j * 128:(j + 1) * 128],
                                    transpose=True).then_inc(s_wt2[g7 % 3], 16)
                    for j in range(CT):
                        s.dma_start(wut[g7 % 3][j][:],
                                    ubfc[c][r0:r0 + 512, j * 128:(j + 1) * 128],
                                    transpose=True).then_inc(s_wu2[g7 % 3], 16)

            @blk.tensor
            def _(t):
                t.wait_ge(s_tc, CT * TT)
                for g7 in range(NG7):
                    t.wait_ge(s_wt2[g7 % 3], 16 * CT * (g7 // 3 + 1))
                    for nt in range(4):
                        nti = 4 * g7 + nt
                        mm = None
                        for tc in range(TC):
                            if nti >= 1:
                                t.wait_ge(s_ge, TC * (nti - 1) + tc + 1)
                            for j in range(CT):
                                mm = t.matmul(ps_G[:, tc * TCW:(tc + 1) * TCW],
                                              wgt[g7 % 3][j][:, nt * 128:(nt + 1) * 128],
                                              actT[:, j * T + tc * TCW: j * T + (tc + 1) * TCW],
                                              start=(j == 0), stop=(j == CT - 1))
                            mm.then_inc(s_gm, 1)
                        if nt == 0:
                            t.wait_ge(s_wu2[g7 % 3], 16 * CT * (g7 // 3 + 1))
                        for tc in range(TC):
                            if nti >= 1:
                                t.wait_ge(s_iv, TC * (nti - 1) + tc + 1)
                            for j in range(CT):
                                mm = t.matmul(ps_U[:, tc * TCW:(tc + 1) * TCW],
                                              wut[g7 % 3][j][:, nt * 128:(nt + 1) * 128],
                                              actT[:, j * T + tc * TCW: j * T + (tc + 1) * TCW],
                                              start=(j == 0), stop=(j == CT - 1))
                            mm.then_inc(s_gm, 1)

            @blk.scalar
            def _(a):
                load_act_set(nc, a, SET_GELU)
                for nti in range(NT):
                    for tc in range(TC):
                        a.wait_ge(s_gm, 2 * TC * nti + tc + 1)
                        if nti >= 2:
                            a.wait_ge(s_iv, TC * (nti - 2) + tc + 1)
                        a.activation(glb[nti % 2][:, tc * TCW:(tc + 1) * TCW],
                                     ps_G[:, tc * TCW:(tc + 1) * TCW],
                                     AF.Gelu_apprx_tanh).then_inc(s_ge, 1)

            @blk.vector
            def _(v):
                for nti in range(NT):
                    for tc in range(TC):
                        v.wait_ge(s_ge, TC * nti + tc + 1)
                        v.wait_ge(s_gm, 2 * TC * nti + TC + tc + 1)
                        if nti >= 2 and tc == 0:
                            v.wait_ge(s_io, 16 * nti)
                        v.tensor_tensor(ib[nti % 2][:, tc * TCW:(tc + 1) * TCW],
                                        glb[nti % 2][:, tc * TCW:(tc + 1) * TCW],
                                        ps_U[:, tc * TCW:(tc + 1) * TCW],
                                        ALU.mult).then_inc(s_iv, 1)

            @blk.gpsimd
            def _(g):
                GPC = (N // 512) // NCH
                ag_at = {4 * GPC * (c - 1) + 3: c for c in range(1, NCH)}
                d_at = 4 * GPC * min(2, NCH - 1) + 3
                for nti in range(NT):
                    row = id_base[nti // 4] + (nti % 4) * 128
                    g.wait_ge(s_iv, TC * (nti + 1))
                    g.dma_start(Id[row:row + 128, :], ib[nti % 2][:]
                                ).then_inc(s_io, 16)
                    if nti in ag_at:
                        c = ag_at[nti]
                        g.collective_compute("AllGather", ALU.bypass, replica_groups=RG,
                                             ins=[gbf_in[c * RPC:(c + 1) * RPC, :].opt()],
                                             outs=[gbfc[c].ap().opt()]).then_inc(s_ccg[c], 1)
                        g.collective_compute("AllGather", ALU.bypass, replica_groups=RG,
                                             ins=[ubf_in[c * RPC:(c + 1) * RPC, :].opt()],
                                             outs=[ubfc[c].ap().opt()]).then_inc(s_ccu[c], 1)
                    if nti == d_at:
                        g.collective_compute("AllGather", ALU.bypass, replica_groups=RG,
                                             ins=[dbf_in.ap().opt()], outs=[dbf.ap().opt()]
                                             ).then_inc(s_ccd, 1)
                g.wait_ge(s_io, 16 * NT)

        b7.close()
        pT.close()

        if stop_after == "B7":
            return nc

        b8 = contextlib.ExitStack()
        b8d = contextlib.ExitStack()
        NT8, NG8 = cfg.NT8, cfg.NG8
        D_sb = [sbT(b8d, f"D_sb_{i}", [128, HH], F32) for i in range(TT)]
        wdt = [[sbT(b8, f"wdt_{p}_{j}", [128, HH], BF16) for j in range(NT8)] for p in range(2)]
        ig = [[sbT(b8, f"ig_{p}_{j}", [128, T], BF16) for j in range(NT8)] for p in range(2)]
        dfull = [sbT(b8, f"dfull_{p}", [128, C], F32) for p in range(2)]
        x2l = sbT(b8, "x2l", [128, C], F32)
        obuf = sbT(b8, "obuf", [128, C], F32)
        ps_d = [psT(b8, f"ps_d_{q}", [128, HH]) for q in range(2)]
        ps_scr = psT(b8, "ps_scr", [128, C])
        s_wt3 = [sem("s_wt30"), sem("s_wt31")]
        s_il = [sem("s_il0"), sem("s_il1")]
        s_dm = sem("s_dm")
        s_da = sem("s_da")
        s_dh = sem("s_dh")
        s_fd = [sem("s_fd0"), sem("s_fd1")]
        s_fx = sem("s_fx")
        s_df = sem("s_df")
        s_oc = sem("s_oc")
        s_out = sem("s_out")
        NIDX = 2 * NG8
        GF = (NIDX - 1) * TT

        with nc.Block() as blk:
            @blk.sync
            def _(s):
                s.wait_ge(s_ccd, 1)
                for idx in range(NIDX):
                    hh, g8 = divmod(idx, NG8)
                    if idx >= 2:
                        s.wait_ge(s_dm, TT * (idx - 1))
                    for j in range(NT8):
                        ncol = g8 * cfg.G8 + j * 128
                        s.dma_start(wdt[idx % 2][j][:],
                                    dbf[hh * HH:(hh + 1) * HH, ncol:ncol + 128],
                                    transpose=True).then_inc(s_wt3[idx % 2], 16)
                s.wait_ge(s_dh, 16 * TT)
                for i in range(TT):
                    if i >= 2:
                        s.wait_ge(s_oc, i - 1)
                    s.dma_start(dfull[i % 2][:, 0:HH], dhalf[i * 128:(i + 1) * 128, :]
                                ).then_inc(s_fd[i % 2], 16)
                    if i >= 1:
                        s.wait_ge(s_oc, i)
                    s.dma_start(x2l[:], x2d[i * 128:(i + 1) * 128, :]
                                ).then_inc(s_fx, 16)

            @blk.gpsimd
            def _(g):
                for idx in range(NIDX):
                    hh, g8 = divmod(idx, NG8)
                    if idx >= 2:
                        g.wait_ge(s_dm, TT * (idx - 1))
                    for j in range(NT8):
                        nrow = g8 * cfg.G8 + j * 128
                        g.dma_start(ig[idx % 2][j][:], Id[nrow:nrow + 128, :]
                                    ).then_inc(s_il[idx % 2], 16)
                    if idx == NG8 - 1:
                        g.wait_ge(s_da, TT * NG8)
                        for i in range(TT):
                            g.dma_start(dhalf[i * 128:(i + 1) * 128, :], D_sb[i][:]
                                        ).then_inc(s_dh, 16)
                for i in range(TT):
                    g.wait_ge(s_oc, i + 1)
                    g.dma_start(out[i * 128:(i + 1) * 128, :], obuf[:]
                                ).then_inc(s_out, 16)
                g.wait_ge(s_out, 16 * TT)

            @blk.tensor
            def _(t):
                for idx in range(NIDX):
                    t.wait_ge(s_wt3[idx % 2], 16 * NT8 * (idx // 2 + 1))
                    t.wait_ge(s_il[idx % 2], 16 * NT8 * (idx // 2 + 1))
                    for i in range(TT):
                        gidx = idx * TT + i
                        if gidx >= 2:
                            t.wait_ge(s_da, gidx - 1)
                        mm = None
                        for j in range(NT8):
                            for hc in range(HC):
                                mm = t.matmul(ps_d[i % 2][:, hc * HCW:(hc + 1) * HCW],
                                              ig[idx % 2][j][:, i * 128:(i + 1) * 128],
                                              wdt[idx % 2][j][:, hc * HCW:(hc + 1) * HCW],
                                              start=(j == 0), stop=(j == NT8 - 1))
                        mm.then_inc(s_dm, 1)

            @blk.scalar
            def _(a):
                load_act_set(nc, a, SET_LN)
                for i in range(TT):
                    col = 3 * TT + i
                    a.wait_ge(s_da, GF + i + 1)
                    a.wait_ge(s_fd[i % 2], 16 * (i // 2 + 1))
                    if i >= 2:
                        a.wait_ge(s_oc, i - 1)
                    a.activation(dfull[i % 2][:, HH:C], D_sb[i][:], AF.Copy
                                 ).then_inc(s_df, 1)
                    ln_a_chain(a, col, dfull[i % 2][:, :], ps_scr[:])

            @blk.vector
            def _(v):
                for idx in range(NIDX):
                    hh, g8 = divmod(idx, NG8)
                    for i in range(TT):
                        gidx = idx * TT + i
                        v.wait_ge(s_dm, gidx + 1)
                        if hh == 1 and g8 == 0 and i == 0:
                            v.wait_ge(s_dh, 16 * TT)
                        if g8 == 0:
                            v.tensor_copy(D_sb[i][:], ps_d[i % 2][:]).then_inc(s_da, 1)
                        else:
                            v.tensor_tensor(D_sb[i][:], D_sb[i][:], ps_d[i % 2][:],
                                            ALU.add).then_inc(s_da, 1)
                        if idx == NIDX - 1:
                            def b9_stats(i):
                                col = 3 * TT + i
                                v.wait_ge(s_df, i + 1)
                                v.wait_ge(s_fd[i % 2], 16 * (i // 2 + 1))
                                if i >= 1:
                                    sP, bP = lnsem(col - 1)
                                    v.wait_ge(sP, bP + 2)
                                ln_v_stats(v, col, dfull[i % 2][:, :], C)

                            def b9_out(m):
                                col = 3 * TT + m
                                ln_v_nmr(v, col)
                                if m >= 1:
                                    v.wait_ge(s_out, 16 * m)
                                ln_v_norm(v, col, obuf[:], dfull[m % 2][:, :])
                                v.wait_ge(s_fx, 16 * (m + 1))
                                v.tensor_tensor(obuf[:], obuf[:],
                                                x2l[:], ALU.add).then_inc(s_oc, 1)

                            b9_stats(i)
                            if i >= 1:
                                b9_out(i - 1)
                            if i == TT - 1:
                                b9_out(i)

        b8.close()
        b8d.close()

    return nc


def _in_maps(cfg, x, W, b, gW, gS, uW, uS, dW, dS):
    maps = []
    for r in range(NCORES):
        maps.append({
            "x": np.ascontiguousarray(x[r]),
            "W_slT": np.ascontiguousarray(W[r * cfg.Ds:(r + 1) * cfg.Ds].T),
            "b": np.ascontiguousarray(b.reshape(1, -1)),
            "gW_sl": np.ascontiguousarray(gW[r * cfg.Ns:(r + 1) * cfg.Ns]),
            "gS_sl": np.ascontiguousarray(gS[r * cfg.Ns:(r + 1) * cfg.Ns]),
            "uW_sl": np.ascontiguousarray(uW[r * cfg.Ns:(r + 1) * cfg.Ns]),
            "uS_sl": np.ascontiguousarray(uS[r * cfg.Ns:(r + 1) * cfg.Ns]),
            "dW_sl": np.ascontiguousarray(dW[r * cfg.Cs:(r + 1) * cfg.Cs]),
            "dS_sl": np.ascontiguousarray(dS[r * cfg.Cs:(r + 1) * cfg.Cs]),
        })
    return maps


def kernel(L0_attn_x, L0_attn_W, L0_attn_b, L0_ffn_gate_W_packed, L0_ffn_gate_scales,
           L0_ffn_up_W_packed, L0_ffn_up_scales, L0_ffn_down_W_packed,
           L0_ffn_down_scales):
    x = np.asarray(L0_attn_x)
    B, T, C = x.shape
    N = L0_ffn_gate_W_packed.shape[0]
    cfg = Cfg(T=T, C=C, N=N)
    nc = build_program(cfg)
    maps = _in_maps(cfg, x, np.asarray(L0_attn_W), np.asarray(L0_attn_b),
                    np.asarray(L0_ffn_gate_W_packed), np.asarray(L0_ffn_gate_scales),
                    np.asarray(L0_ffn_up_W_packed), np.asarray(L0_ffn_up_scales),
                    np.asarray(L0_ffn_down_W_packed), np.asarray(L0_ffn_down_scales))
    res = run_bass_kernel_spmd(nc, maps, core_ids=list(range(NCORES)))
    return np.stack([res.results[r]["out"] for r in range(NCORES)], axis=0)


if __name__ == "__main__":
    pass


# version 27
# speedup vs baseline: 1.0186x; 1.0186x over previous
import contextlib
import numpy as np
import concourse.bass as bass
import concourse.mybir as mybir
from concourse.bass_utils import run_bass_kernel_spmd

F32 = mybir.dt.float32
BF16 = mybir.dt.bfloat16
I32 = mybir.dt.int32
U8 = mybir.dt.uint8
AF = mybir.ActivationFunctionType
ALU = mybir.AluOpType
AX = mybir.AxisListType

NCORES = 8
SET_LN = 6
SET_SIG = 2
SET_GELU = 11
LNP = 16


def load_act_set(nc, eng, idx):
    eng.add_instruction(mybir.InstLoadActFuncSet(
        name=nc.get_next_instruction_name(), act_func_set_id=idx, ins=[], outs=[]))


class Cfg:
    def __init__(self, T, C, N):
        assert T % 128 == 0 and C % 256 == 0 and N % 1024 == 0
        self.T, self.C, self.N = T, C, N
        self.D = 2 * C
        self.TT = T // 128
        self.CT = C // 128
        self.NT = N // 128
        self.TCW = min(T, 512)
        self.TC = T // self.TCW
        self.JU = C // 2
        self.BU = C // 32
        self.Ds = self.D // NCORES
        self.Ns = self.N // NCORES
        self.Cs = self.C // NCORES
        self.NG7 = N // 512
        self.HH = C // 2
        self.HCW = min(self.HH, 512)
        self.HC = self.HH // self.HCW
        self.G8 = min(N, 1024)
        self.NG8 = N // self.G8
        self.NT8 = self.G8 // 128


def build_program(cfg, stop_after=None):
    T, C, D, N = cfg.T, cfg.C, cfg.D, cfg.N
    TT, CT, NT = cfg.TT, cfg.CT, cfg.NT
    TC, TCW = cfg.TC, cfg.TCW
    HH, HC, HCW = cfg.HH, cfg.HC, cfg.HCW
    JU, BU = cfg.JU, cfg.BU
    nc = bass.Bass("TRN2", target_bir_lowering=False, detect_race_conditions=False)

    x_in = nc.dram_tensor("x", [T, C], F32, kind="ExternalInput")
    W_slT = nc.dram_tensor("W_slT", [C, cfg.Ds], F32, kind="ExternalInput")
    b_in = nc.dram_tensor("b", [1, D], F32, kind="ExternalInput")
    gW_sl = nc.dram_tensor("gW_sl", [cfg.Ns, C // 2], I32, kind="ExternalInput")
    gS_sl = nc.dram_tensor("gS_sl", [cfg.Ns, C // 32], F32, kind="ExternalInput")
    uW_sl = nc.dram_tensor("uW_sl", [cfg.Ns, C // 2], I32, kind="ExternalInput")
    uS_sl = nc.dram_tensor("uS_sl", [cfg.Ns, C // 32], F32, kind="ExternalInput")
    dW_sl = nc.dram_tensor("dW_sl", [cfg.Cs, N // 2], I32, kind="ExternalInput")
    dS_sl = nc.dram_tensor("dS_sl", [cfg.Cs, N // 32], F32, kind="ExternalInput")
    out = nc.dram_tensor("out", [T, C], F32, kind="ExternalOutput")

    Wbf_in = nc.dram_tensor("Wbf_in", [C, cfg.Ds], BF16)
    Wbf = nc.dram_tensor("Wbf", [NCORES * C, cfg.Ds], BF16)
    gbf_in = nc.dram_tensor("gbf_in", [cfg.Ns, C], BF16)
    NCH = max(1, (N // NCORES) // 512)
    RPC = (N // NCORES) // NCH
    gbfc = [nc.dram_tensor(f"gbf{c}", [RPC * NCORES, C], BF16) for c in range(NCH)]
    ubf_in = nc.dram_tensor("ubf_in", [cfg.Ns, C], BF16)
    ubfc = [nc.dram_tensor(f"ubf{c}", [RPC * NCORES, C], BF16) for c in range(NCH)]
    dbf_in = nc.dram_tensor("dbf_in", [cfg.Cs, N], BF16)
    dbf = nc.dram_tensor("dbf", [C, N], BF16)
    ln1d = nc.dram_tensor("ln1d", [T, C], BF16)
    hhi = nc.dram_tensor("hhi", [C, T], BF16)
    hlo = nc.dram_tensor("hlo", [C, T], BF16)
    x2d = nc.dram_tensor("x2d", [T, C], F32)
    Id = nc.dram_tensor("Id", [N, T], BF16)
    dhalf = nc.dram_tensor("dhalf", [T, HH], F32)

    top = contextlib.ExitStack()
    sem = lambda name: top.enter_context(nc.semaphore(name))
    sbT = lambda st, name, shape, dt: st.enter_context(nc.sbuf_tensor(name, shape, dt))
    psT = lambda st, name, shape: st.enter_context(nc.psum_tensor(name, shape, F32))

    with top:
        c_m1 = sbT(top, "c_m1", [128, 1], F32)
        c_75 = sbT(top, "c_75", [128, 1], F32)
        c_eps = sbT(top, "c_eps", [128, 1], F32)
        c_mh = sbT(top, "c_mh", [128, 1], F32)
        c_zero = sbT(top, "c_zero", [128, 1], F32)
        ones_bf = sbT(top, "ones_bf", [1, 512], BF16)
        b_bf = sbT(top, "b_bf", [1, D], BF16)
        NCOL = 4 * TT
        st_sum = sbT(top, "st_sum", [128, NCOL], F32)
        st_sq = sbT(top, "st_sq", [128, NCOL], F32)
        st_m = sbT(top, "st_m", [128, NCOL], F32)
        st_v0 = sbT(top, "st_v0", [128, NCOL], F32)
        st_nv = sbT(top, "st_nv", [128, NCOL], F32)
        st_ln = sbT(top, "st_ln", [128, NCOL], F32)
        st_r = sbT(top, "st_r", [128, NCOL], F32)
        st_nmr = sbT(top, "st_nmr", [128, NCOL], F32)
        idm = sbT(top, "idm", [128, 128], BF16)

        ln_sems = [sem(f"s_ln{i}") for i in range(LNP)]

        def lnsem(col):
            return ln_sems[col % LNP], 8 * (col // LNP)

        def ln_v_stats(v, col, src_ap, F):
            s, base = lnsem(col)
            cs = slice(col, col + 1)
            v.reduce_sum(st_sum[:, cs], src_ap, AX.X).then_inc(s, 1)
            v.wait_ge(s, base + 2)
            v.tensor_scalar(st_m[:, cs], st_sum[:, cs], 1.0 / F, None, ALU.mult).then_inc(s, 1)
            v.tensor_scalar(st_v0[:, cs], st_sq[:, cs], 1.0 / F, None, ALU.mult).then_inc(s, 1)
            v.wait_ge(s, base + 4)
            v.tensor_scalar(st_nv[:, cs], st_m[:, cs], st_m[:, cs], st_v0[:, cs],
                            ALU.mult, ALU.subtract).then_inc(s, 1)

        def ln_v_nmr(v, col):
            s, base = lnsem(col)
            cs = slice(col, col + 1)
            v.wait_ge(s, base + 7)
            v.scalar_tensor_tensor(st_nmr[:, cs], st_m[:, cs], -1.0, st_r[:, cs],
                                   ALU.mult, ALU.mult).then_inc(s, 1)

        def ln_v_norm(v, col, out_ap, src_ap):
            s, base = lnsem(col)
            cs = slice(col, col + 1)
            v.wait_ge(s, base + 8)
            return v.tensor_scalar(out_ap, src_ap, st_r[:, cs], st_nmr[:, cs],
                                   ALU.mult, ALU.add)

        def ln_a_chain(a, col, src_ap, scr_ap):
            s, base = lnsem(col)
            cs = slice(col, col + 1)
            a.activation(scr_ap, src_ap, AF.Square, accum_out=st_sq[:, cs]).then_inc(s, 1)
            a.wait_ge(s, base + 5)
            a.activation(st_ln[:, cs], st_nv[:, cs], AF.Ln,
                         bias=c_eps[:], scale=c_m1[:]).then_inc(s, 1)
            a.wait_ge(s, base + 6)
            a.activation(st_r[:, cs], st_ln[:, cs], AF.Exp,
                         bias=c_zero[:], scale=c_mh[:]).then_inc(s, 1)

        units = []
        for nt in range(cfg.Ns // 128):
            units.append((gW_sl, gS_sl, gbf_in, nt, 0))
        NU_G = len(units)
        for nt in range(cfg.Ns // 128):
            units.append((uW_sl, uS_sl, ubf_in, nt, 0))
        NU_U = len(units)
        for nt in range(cfg.Cs // 128):
            for jc in range((N // 2) // JU):
                units.append((dW_sl, dS_sl, dbf_in, nt, jc))
        NU = len(units)

        bi = contextlib.ExitStack()
        p8 = [sbT(bi, f"p8_{i}", [128, JU], U8) for i in range(2)]
        scs = sbT(bi, "scs", [128, NU * BU], F32)
        c8 = sbT(bi, "c8", [128, C], U8)
        cf = sbT(bi, "cf", [128, C], BF16)
        sgn = sbT(bi, "sgn", [128, C], BF16)
        dxv = sbT(bi, "dxv", [128, C], BF16)
        dt3 = sbT(bi, "dt3", [128, C], BF16)
        wout = [sbT(bi, f"wout_{i}", [128, C], BF16) for i in range(2)]
        itf = sbT(bi, "itf", [128, 128], I32)
        itp = sbT(bi, "itp", [128, 128], I32)
        xbuf = [sbT(bi, f"xbuf_{i}", [128, C], F32) for i in range(2)]
        lnbuf = [sbT(bi, f"lnbuf_{i}", [128, C], BF16) for i in range(2)]
        wtg = [sbT(bi, f"wtg_{p}", [128, CT * 256], BF16) for p in range(2)]
        wtv = [sbT(bi, f"wtv_{p}", [128, CT * 256], BF16) for p in range(2)]
        gsb = sbT(bi, "gsb", [128, T], F32)
        vsb = sbT(bi, "vsb", [128, T], F32)
        hsb = sbT(bi, "hsb", [128, T], F32)
        hib = sbT(bi, "hib", [128, T], BF16)
        hlb = sbT(bi, "hlb", [128, T], BF16)
        pT = contextlib.ExitStack()
        actT = pT.enter_context(nc.sbuf_tensor("actT", [128, CT * T], BF16, side="right"))
        psA = contextlib.ExitStack()
        PW = max(T, C)
        ps_g = psT(psA, "ps_g", [128, PW])
        ps_v = psT(psA, "ps_v", [128, PW])

        s_prep = sem("s_prep")
        s_ccw = sem("s_ccw")
        s_ccg = [sem(f"s_ccg{c}") for c in range(NCH)]
        s_ccu = [sem(f"s_ccu{c}") for c in range(NCH)]
        s_ccd = sem("s_ccd")
        s_pk = [sem("s_pk0"), sem("s_pk1")]
        s_sc = sem("s_sc")
        s_nib = sem("s_nib")
        s_dec = sem("s_dec")
        s_wo = sem("s_wo")
        s_xl = [sem("s_xl0"), sem("s_xl1")]
        s_n1 = sem("s_n1")
        s_o1 = sem("s_o1")
        s_t1 = sem("s_t1")
        s_wt = [sem("s_wt0"), sem("s_wt1")]
        s_mm = sem("s_mm")
        s_ev = sem("s_ev")
        s_hv = sem("s_hv")
        s_hs = sem("s_hs")
        s_gm = sem("s_gm")
        s_im = sem("s_im")
        RG = [list(range(NCORES))]

        with nc.Block() as blk:
            @blk.gpsimd
            def _(g):
                g.iota(itf[:], [[1, 128]], base=0, channel_multiplier=0).then_inc(s_im, 1)
                g.iota(itp[:], [[0, 128]], base=0, channel_multiplier=1).then_inc(s_im, 1)
                g.memset(c_m1[:], -1.0)
                g.memset(c_75[:], 7.5)
                g.memset(c_eps[:], 1e-5)
                g.memset(c_mh[:], -0.5)
                g.memset(c_zero[:], 0.0)
                g.memset(ones_bf[:], 1.0)
                g.dma_start(Wbf_in[:, :], W_slT[:, :]).then_inc(s_prep, 16)
                g.dma_start(b_bf[:, :], b_in[:, :]).then_inc(s_prep, 16)
                g.wait_ge(s_prep, 16)
                g.collective_compute("AllGather", ALU.bypass, replica_groups=RG,
                                     ins=[Wbf_in.ap().opt()], outs=[Wbf.ap().opt()]
                                     ).then_inc(s_ccw, 1)

                def emit_out(i):
                    _, _, dst, nt, jc = units[i]
                    g.wait_ge(s_dec, i + 1)
                    g.dma_start(dst[nt * 128:(nt + 1) * 128, jc * C:(jc + 1) * C],
                                wout[i % 2][:]).then_inc(s_wo, 16)


                for i, (src, ssrc, dst, nt, jc) in enumerate(units):
                    if i >= 2:
                        g.wait_ge(s_nib, i - 1)
                    g.dma_start(p8[i % 2][:],
                                src[nt * 128:(nt + 1) * 128, jc * JU:(jc + 1) * JU]
                                ).then_inc(s_pk[i % 2], 16)
                    if i >= 1:
                        emit_out(i - 1)
                emit_out(NU - 1)
                g.wait_ge(s_wo, 16 * NU)
                g.wait_ge(s_hs, 32 * CT)
                g.collective_compute("AllGather", ALU.bypass, replica_groups=RG,
                                     ins=[gbf_in[0:RPC, :].opt()],
                                     outs=[gbfc[0].ap().opt()]).then_inc(s_ccg[0], 1)
                g.collective_compute("AllGather", ALU.bypass, replica_groups=RG,
                                     ins=[ubf_in[0:RPC, :].opt()],
                                     outs=[ubfc[0].ap().opt()]).then_inc(s_ccu[0], 1)

            @blk.sync
            def _(s):
                for i in range(TT):
                    if i >= 2:
                        s.wait_ge(s_n1, i - 1)
                    s.dma_start(xbuf[i % 2][:], x_in[i * 128:(i + 1) * 128, :]
                                ).then_inc(s_xl[i % 2], 16)
                for i, (src, ssrc, dst, nt, jc) in enumerate(units):
                    s.dma_start(scs[:, i * BU:(i + 1) * BU],
                                ssrc[nt * 128:(nt + 1) * 128, jc * BU:(jc + 1) * BU]
                                ).then_inc(s_sc, 16)
                s.wait_ge(s_ccw, 1)

                def wt_sp(q):
                    if q >= 2:
                        s.wait_ge(s_mm, 4 * (q - 1))
                    fg = q * 256
                    rg, cg = fg // cfg.Ds, fg % cfg.Ds
                    fv = C + q * 256
                    rv, cv_ = fv // cfg.Ds, fv % cfg.Ds
                    for j in range(CT):
                        s.dma_start(wtg[q % 2][:, j * 256:(j + 1) * 256],
                                    Wbf[rg * C + j * 128:rg * C + (j + 1) * 128,
                                        cg:cg + 256]).then_inc(s_wt[q % 2], 16)
                        s.dma_start(wtv[q % 2][:, j * 256:(j + 1) * 256],
                                    Wbf[rv * C + j * 128:rv * C + (j + 1) * 128,
                                        cv_:cv_ + 256]).then_inc(s_wt[q % 2], 16)

                NQ = CT // 2
                wt_sp(0)
                wt_sp(1)
                s.wait_ge(s_o1, 16 * TT)
                for j in range(CT):
                    s.dma_start(actT[:, j * T:(j + 1) * T],
                                ln1d[:, j * 128:(j + 1) * 128], transpose=True
                                ).then_inc(s_t1, 16)
                for q in range(2, NQ):
                    wt_sp(q)

            @blk.vector
            def _(v):
                v.wait_ge(s_im, 2)
                v.tensor_tensor(idm[:], itf[:], itp[:], ALU.is_equal).then_inc(s_im, 1)

                def a1_tile(i):
                    v.wait_ge(s_xl[i % 2], 16 * (i // 2 + 1))
                    ln_v_stats(v, i, xbuf[i % 2][:, :], C)
                    ln_v_nmr(v, i)
                    if i >= 2:
                        v.wait_ge(s_o1, 16 * i)
                    ln_v_norm(v, i, lnbuf[i % 2][:], xbuf[i % 2][:]).then_inc(s_n1, 1)

                def dec_unit(i):
                    pb = p8[i % 2]
                    v.wait_ge(s_pk[i % 2], 16 * (i // 2 + 1))
                    cv = c8.ap().rearrange("a (j two) -> a j two", two=2)
                    v.tensor_scalar(cv[:, :, 0:1], pb.ap().unsqueeze(2), 15, None,
                                    ALU.bitwise_and)
                    v.tensor_scalar(cv[:, :, 1:2], pb.ap().unsqueeze(2), 4, None,
                                    ALU.logical_shift_right)
                    v.tensor_copy(cf[:], c8[:]).then_inc(s_nib, 1)
                    sg = sgn
                    v.tensor_scalar(sg[:], cf[:], -2.0, 15.0, ALU.mult, ALU.add)
                    v.tensor_scalar(sg[:], sg[:], -1.0, None, ALU.max)
                    v.tensor_scalar(sg[:], sg[:], 1.0, None, ALU.min)
                    v.scalar_tensor_tensor(dxv[:], sg[:], 4.0, cf[:], ALU.mult, ALU.add)
                    v.tensor_scalar(cf[:], dxv[:], 0.5, -2.0, ALU.mult, ALU.add)
                    v.scalar_tensor_tensor(cf[:], dxv[:], 6.0, cf[:], ALU.subtract, ALU.max)
                    v.tensor_scalar(dt3[:], dxv[:], 2.0, -16.0, ALU.mult, ALU.add)
                    v.tensor_tensor(dt3[:], cf[:], dt3[:], ALU.max)
                    v.tensor_tensor(dxv[:], dt3[:], sg[:], ALU.mult)
                    if i >= 2:
                        v.wait_ge(s_wo, 16 * i)
                    if i == 0:
                        v.wait_ge(s_sc, 16 * NU)
                    vv = dxv.ap().rearrange("a (b k) -> a b k", k=32)
                    wv = wout[i % 2].ap().rearrange("a (b k) -> a b k", k=32)
                    scb = scs[:, i * BU:(i + 1) * BU].unsqueeze(2).broadcast_to([128, BU, 32])
                    v.tensor_tensor(wv, vv, scb, ALU.mult).then_inc(s_dec, 1)

                def scan_pair(k):
                    v.wait_ge(s_ev, 2 * (k + 1))
                    if k >= 1:
                        v.wait_ge(s_hs, 32 * k)
                    v.tensor_tensor_scan(hsb[:], gsb[:], vsb[:], 0.0, ALU.mult, ALU.add)
                    v.tensor_copy(hib[:], hsb[:])
                    v.tensor_tensor(hlb[:], hsb[:], hib[:], ALU.subtract).then_inc(s_hv, 1)

                for i in range(TT):
                    a1_tile(i)
                ui = 0

                def take(n):
                    nonlocal ui
                    stop = min(ui + n, NU)
                    while ui < stop:
                        dec_unit(ui)
                        ui += 1

                take(2)
                for k in range(CT):
                    scan_pair(k)
                    take(3)
                take(NU)

            @blk.scalar
            def _(a):
                load_act_set(nc, a, SET_LN)
                for i in range(TT):
                    a.wait_ge(s_xl[i % 2], 16 * (i // 2 + 1))
                    ln_a_chain(a, i, xbuf[i % 2][:, :], ps_g[:, 0:C])
                    a.wait_ge(s_n1, i + 1)
                    nc.scalar.dma_start(ln1d[i * 128:(i + 1) * 128, :], lnbuf[i % 2][:]
                                        ).then_inc(s_o1, 16)
                load_act_set(nc, a, SET_SIG)
                for k in range(CT):
                    if k >= 1:
                        a.wait_ge(s_hv, k)
                        nc.scalar.dma_start(hhi[(k - 1) * 128:k * 128, :], hib[:]
                                            ).then_inc(s_hs, 16)
                        nc.scalar.dma_start(hlo[(k - 1) * 128:k * 128, :], hlb[:]
                                            ).then_inc(s_hs, 16)
                    a.wait_ge(s_mm, 2 * k + 1)
                    if k >= 1:
                        a.wait_ge(s_hv, k)
                    a.activation(gsb[:], ps_g[:, 0:T], AF.Sigmoid).then_inc(s_ev, 1)
                    a.wait_ge(s_mm, 2 * k + 2)
                    a.activation(vsb[:], ps_v[:, 0:T], AF.Tanh).then_inc(s_ev, 1)
                a.wait_ge(s_hv, CT)
                nc.scalar.dma_start(hhi[(CT - 1) * 128:CT * 128, :], hib[:]).then_inc(s_hs, 16)
                nc.scalar.dma_start(hlo[(CT - 1) * 128:CT * 128, :], hlb[:]).then_inc(s_hs, 16)

            @blk.tensor
            def _(t):
                t.wait_ge(s_t1, 16 * CT)
                t.wait_ge(s_prep, 32)
                for k in range(CT):
                    q, kq = divmod(k, 2)
                    if kq == 0:
                        t.wait_ge(s_wt[q % 2], 32 * CT * (q // 2 + 1))
                    for psum, wtiles, boff in ((ps_g, wtg[q % 2], 0), (ps_v, wtv[q % 2], C)):
                        if k >= 1:
                            t.wait_ge(s_ev, 2 * k - (1 if boff == 0 else 0))
                        for j in range(CT):
                            for tc in range(TC):
                                t.matmul(psum[:, tc * TCW:(tc + 1) * TCW],
                                         wtiles[:, j * 256 + kq * 128:
                                                j * 256 + (kq + 1) * 128],
                                         actT[:, j * T + tc * TCW: j * T + (tc + 1) * TCW],
                                         start=(j == 0), stop=False)
                        mm = None
                        for tc in range(TC):
                            mm = t.matmul(psum[:, tc * TCW:(tc + 1) * TCW],
                                          b_bf[0:1, boff + k * 128: boff + (k + 1) * 128],
                                          ones_bf[0:1, 0:TCW], start=False, stop=True)
                        mm.then_inc(s_mm, 1)

        bi.close()
        psA.close()

        a4 = contextlib.ExitStack()
        psB = contextlib.ExitStack()
        ps2 = [psB.enter_context(nc.psum_tensor(f"ps2_{p}", [128, C], BF16))
               for p in range(2)]
        ps_h = psB.enter_context(nc.psum_tensor("ps_h", [128, C], BF16))
        ps_l = psB.enter_context(nc.psum_tensor("ps_l", [128, C], BF16))
        xbuf2 = [sbT(a4, f"xbuf2_{p}", [128, C], F32) for p in range(4)]
        lnbuf2 = [sbT(a4, f"lnbuf2_{p}", [128, C], BF16) for p in range(2)]
        hif = [sbT(a4, f"hif_{p}", [128, C], BF16) for p in range(2)]
        lof = [sbT(a4, f"lof_{p}", [128, C], BF16) for p in range(2)]
        hrow = [sbT(a4, f"hrow_{p}", [128, C], F32) for p in range(2)]
        lsb = sbT(a4, "lsb", [128, C], BF16)
        alnb = sbT(a4, "alnb", [128, C], F32)
        x2b = [sbT(a4, f"x2b_{p}", [128, C], F32) for p in range(2)]
        scr4 = sbT(a4, "scr4", [128, C], F32)
        s_hf = [sem("s_hf0"), sem("s_hf1")]
        s_xl2 = [sem(f"s_xl2{p}") for p in range(4)]
        s_thm = sem("s_thm")
        s_hr = sem("s_hr")
        s_n2 = sem("s_n2")
        s_o2x = [sem("s_o2x0"), sem("s_o2x1")]
        s_tm = sem("s_tm")
        s_tc = sem("s_tc")

        hhi_v = hhi.ap().rearrange("(k p) t -> p k t", p=128)
        hlo_v = hlo.ap().rearrange("(k p) t -> p k t", p=128)

        with nc.Block() as blk:
            @blk.sync
            def _(s):
                s.wait_ge(s_hs, 32 * CT)
                for i in range(TT):
                    if i >= 2:
                        s.wait_ge(s_thm, i - 1)
                    if i >= 4:
                        s.wait_ge(s_n2, 2 * (i - 4) + 1)
                    dst = hif[i % 2].ap().rearrange("p (k t) -> p k t", t=128)
                    s.dma_start(dst, hhi_v[:, :, i * 128:(i + 1) * 128]
                                ).then_inc(s_hf[i % 2], 16)
                    dst = lof[i % 2].ap().rearrange("p (k t) -> p k t", t=128)
                    s.dma_start(dst, hlo_v[:, :, i * 128:(i + 1) * 128]
                                ).then_inc(s_hf[i % 2], 16)
                    s.dma_start(xbuf2[i % 4][:], x_in[i * 128:(i + 1) * 128, :]
                                ).then_inc(s_xl2[i % 4], 16)

            @blk.tensor
            def _(t):
                t.wait_ge(s_im, 3)
                for i in range(TT + 2):
                    if i < TT:
                        t.wait_ge(s_hf[i % 2], 32 * (i // 2 + 1))
                        if i >= 1:
                            t.wait_ge(s_hr, i)
                        mm = None
                        for k in range(CT):
                            mm = t.transpose(ps_h[:, k * 128:(k + 1) * 128],
                                             hif[i % 2][:, k * 128:(k + 1) * 128],
                                             idm[:])
                        for k in range(CT):
                            mm = t.transpose(ps_l[:, k * 128:(k + 1) * 128],
                                             lof[i % 2][:, k * 128:(k + 1) * 128],
                                             idm[:])
                        mm.then_inc(s_thm, 1)
                    if i >= 2:
                        m = i - 2
                        t.wait_ge(s_n2, 2 * m + 2)
                        if m >= 2:
                            t.wait_ge(s_tc, CT * (m - 1))
                        mm = None
                        for j in range(CT):
                            mm = t.transpose(ps2[m % 2][:, j * 128:(j + 1) * 128],
                                             lnbuf2[m % 2][:, j * 128:(j + 1) * 128],
                                             idm[:])
                        mm.then_inc(s_tm, 1)

            @blk.scalar
            def _(a):
                load_act_set(nc, a, SET_LN)
                for i in range(TT):
                    ca, cb_ = TT + 2 * i, TT + 2 * i + 1
                    a.wait_ge(s_hr, i + 1)
                    ln_a_chain(a, ca, hrow[i % 2][:, :], scr4[:])
                    a.wait_ge(s_n2, 2 * i + 1)
                    ln_a_chain(a, cb_, x2b[i % 2][:, :], scr4[:])
                    if i >= 2:
                        a.wait_ge(s_tm, i - 1)
                        m = i - 2
                        for j in range(CT):
                            a.activation(actT[:, j * T + m * 128:j * T + (m + 1) * 128],
                                         ps2[m % 2][:, j * 128:(j + 1) * 128],
                                         AF.Copy).then_inc(s_tc, 1)
                for m in (TT - 2, TT - 1):
                    a.wait_ge(s_tm, m + 1)
                    for j in range(CT):
                        a.activation(actT[:, j * T + m * 128:j * T + (m + 1) * 128],
                                     ps2[m % 2][:, j * 128:(j + 1) * 128],
                                     AF.Copy).then_inc(s_tc, 1)

            @blk.vector
            def _(v):
                for i in range(TT):
                    ca, cb_ = TT + 2 * i, TT + 2 * i + 1
                    v.wait_ge(s_thm, i + 1)
                    v.tensor_copy(lsb[:], ps_l[:])
                    v.tensor_tensor(hrow[i % 2][:], ps_h[:], lsb[:],
                                    ALU.add).then_inc(s_hr, 1)
                    ln_v_stats(v, ca, hrow[i % 2][:, :], C)
                    ln_v_nmr(v, ca)
                    sA, baseA = lnsem(ca)
                    v.wait_ge(sA, baseA + 8)
                    v.tensor_scalar(alnb[:], hrow[i % 2][:], st_r[:, ca:ca + 1],
                                    st_nmr[:, ca:ca + 1], ALU.mult, ALU.add)
                    v.wait_ge(s_xl2[i % 4], 16 * (i // 4 + 1))
                    if i >= 2:
                        v.wait_ge(s_o2x[i % 2], 16 * (i // 2))
                    v.tensor_tensor(x2b[i % 2][:], alnb[:], xbuf2[i % 4][:],
                                    ALU.add).then_inc(s_n2, 1)
                    ln_v_stats(v, cb_, x2b[i % 2][:, :], C)
                    ln_v_nmr(v, cb_)
                    if i >= 2:
                        v.wait_ge(s_tm, i - 1)
                    ln_v_norm(v, cb_, lnbuf2[i % 2][:], x2b[i % 2][:]).then_inc(s_n2, 1)

            @blk.gpsimd
            def _(g):
                for i in range(TT):
                    g.wait_ge(s_n2, 2 * i + 1)
                    g.dma_start(x2d[i * 128:(i + 1) * 128, :], x2b[i % 2][:]).then_inc(s_o2x[i % 2], 16)

        a4.close()
        psB.close()

        if stop_after == "BI":
            return nc

        b7 = contextlib.ExitStack()
        wgt = [[sbT(b7, f"wgt_{p}_{j}", [128, 512], BF16) for j in range(CT)] for p in range(3)]
        wut = [[sbT(b7, f"wut_{p}_{j}", [128, 512], BF16) for j in range(CT)] for p in range(3)]
        glb = [sbT(b7, f"glb_{p}", [128, T], F32) for p in range(2)]
        ib = [sbT(b7, f"ib_{p}", [128, T], BF16) for p in range(2)]
        ps_G = psT(b7, "ps_G", [128, T])
        ps_U = psT(b7, "ps_U", [128, T])
        s_wt2 = [sem("s_wt20"), sem("s_wt21"), sem("s_wt22")]
        s_wu2 = [sem("s_wu20"), sem("s_wu21"), sem("s_wu22")]
        s_ge = sem("s_ge")
        s_iv = sem("s_iv")
        s_io = sem("s_io")
        NG7 = cfg.NG7

        g7order = [(c, r) for c in range(NCH) for r in range(NCORES)]
        assert len(g7order) * 512 == N
        id_base = [r * cfg.Ns + c * RPC for (c, r) in g7order]

        with nc.Block() as blk:
            @blk.sync
            def _(s):
                for g7, (c, r) in enumerate(g7order):
                    if r == 0:
                        s.wait_ge(s_ccg[c], 1)
                        s.wait_ge(s_ccu[c], 1)
                    if g7 >= 3:
                        s.wait_ge(s_gm, 8 * TC * (g7 - 2))
                    r0 = r * RPC
                    for j in range(CT):
                        s.dma_start(wgt[g7 % 3][j][:],
                                    gbfc[c][r0:r0 + 512, j * 128:(j + 1) * 128],
                                    transpose=True).then_inc(s_wt2[g7 % 3], 16)
                    for j in range(CT):
                        s.dma_start(wut[g7 % 3][j][:],
                                    ubfc[c][r0:r0 + 512, j * 128:(j + 1) * 128],
                                    transpose=True).then_inc(s_wu2[g7 % 3], 16)

            @blk.tensor
            def _(t):
                t.wait_ge(s_tc, CT * TT)
                for g7 in range(NG7):
                    t.wait_ge(s_wt2[g7 % 3], 16 * CT * (g7 // 3 + 1))
                    for nt in range(4):
                        nti = 4 * g7 + nt
                        mm = None
                        for tc in range(TC):
                            if nti >= 1:
                                t.wait_ge(s_ge, TC * (nti - 1) + tc + 1)
                            for j in range(CT):
                                mm = t.matmul(ps_G[:, tc * TCW:(tc + 1) * TCW],
                                              wgt[g7 % 3][j][:, nt * 128:(nt + 1) * 128],
                                              actT[:, j * T + tc * TCW: j * T + (tc + 1) * TCW],
                                              start=(j == 0), stop=(j == CT - 1))
                            mm.then_inc(s_gm, 1)
                        if nt == 0:
                            t.wait_ge(s_wu2[g7 % 3], 16 * CT * (g7 // 3 + 1))
                        for tc in range(TC):
                            if nti >= 1:
                                t.wait_ge(s_iv, TC * (nti - 1) + tc + 1)
                            for j in range(CT):
                                mm = t.matmul(ps_U[:, tc * TCW:(tc + 1) * TCW],
                                              wut[g7 % 3][j][:, nt * 128:(nt + 1) * 128],
                                              actT[:, j * T + tc * TCW: j * T + (tc + 1) * TCW],
                                              start=(j == 0), stop=(j == CT - 1))
                            mm.then_inc(s_gm, 1)

            @blk.scalar
            def _(a):
                load_act_set(nc, a, SET_GELU)
                for nti in range(NT):
                    for tc in range(TC):
                        a.wait_ge(s_gm, 2 * TC * nti + tc + 1)
                        if nti >= 2:
                            a.wait_ge(s_iv, TC * (nti - 2) + tc + 1)
                        a.activation(glb[nti % 2][:, tc * TCW:(tc + 1) * TCW],
                                     ps_G[:, tc * TCW:(tc + 1) * TCW],
                                     AF.Gelu_apprx_tanh).then_inc(s_ge, 1)

            @blk.vector
            def _(v):
                for nti in range(NT):
                    for tc in range(TC):
                        v.wait_ge(s_ge, TC * nti + tc + 1)
                        v.wait_ge(s_gm, 2 * TC * nti + TC + tc + 1)
                        if nti >= 2 and tc == 0:
                            v.wait_ge(s_io, 16 * nti)
                        v.tensor_tensor(ib[nti % 2][:, tc * TCW:(tc + 1) * TCW],
                                        glb[nti % 2][:, tc * TCW:(tc + 1) * TCW],
                                        ps_U[:, tc * TCW:(tc + 1) * TCW],
                                        ALU.mult).then_inc(s_iv, 1)

            @blk.gpsimd
            def _(g):
                GPC = (N // 512) // NCH
                ag_at = {4 * GPC * (c - 1) + 3: c for c in range(1, NCH)}
                d_at = 4 * GPC * min(2, NCH - 1) + 3
                for nti in range(NT):
                    row = id_base[nti // 4] + (nti % 4) * 128
                    g.wait_ge(s_iv, TC * (nti + 1))
                    g.dma_start(Id[row:row + 128, :], ib[nti % 2][:]
                                ).then_inc(s_io, 16)
                    if nti in ag_at:
                        c = ag_at[nti]
                        g.collective_compute("AllGather", ALU.bypass, replica_groups=RG,
                                             ins=[gbf_in[c * RPC:(c + 1) * RPC, :].opt()],
                                             outs=[gbfc[c].ap().opt()]).then_inc(s_ccg[c], 1)
                        g.collective_compute("AllGather", ALU.bypass, replica_groups=RG,
                                             ins=[ubf_in[c * RPC:(c + 1) * RPC, :].opt()],
                                             outs=[ubfc[c].ap().opt()]).then_inc(s_ccu[c], 1)
                    if nti == d_at:
                        g.collective_compute("AllGather", ALU.bypass, replica_groups=RG,
                                             ins=[dbf_in.ap().opt()], outs=[dbf.ap().opt()]
                                             ).then_inc(s_ccd, 1)
                g.wait_ge(s_io, 16 * NT)

        b7.close()
        pT.close()

        if stop_after == "B7":
            return nc

        b8 = contextlib.ExitStack()
        b8d = contextlib.ExitStack()
        NT8, NG8 = cfg.NT8, cfg.NG8
        D_sb = [sbT(b8d, f"D_sb_{i}", [128, HH], F32) for i in range(TT)]
        wdt = [[sbT(b8, f"wdt_{p}_{j}", [128, HH], BF16) for j in range(NT8)] for p in range(2)]
        ig = [[sbT(b8, f"ig_{p}_{j}", [128, T], BF16) for j in range(NT8)] for p in range(2)]
        dfull = [sbT(b8, f"dfull_{p}", [128, C], F32) for p in range(2)]
        x2l = sbT(b8, "x2l", [128, C], F32)
        obuf = sbT(b8, "obuf", [128, C], F32)
        ps_d = [psT(b8, f"ps_d_{q}", [128, HH]) for q in range(3)]
        scr8 = sbT(b8, "scr8", [128, C], BF16)
        s_wt3 = [sem("s_wt30"), sem("s_wt31")]
        s_il = [sem("s_il0"), sem("s_il1")]
        s_dm = sem("s_dm")
        s_da = sem("s_da")
        s_dh = sem("s_dh")
        s_fd = [sem("s_fd0"), sem("s_fd1")]
        s_fx = sem("s_fx")
        s_df = sem("s_df")
        s_oc = sem("s_oc")
        s_out = sem("s_out")
        NIDX = 2 * NG8
        GF = (NIDX - 1) * TT

        with nc.Block() as blk:
            @blk.sync
            def _(s):
                s.wait_ge(s_ccd, 1)
                for idx in range(NIDX):
                    hh, g8 = divmod(idx, NG8)
                    if idx >= 2:
                        s.wait_ge(s_dm, TT * (idx - 1))
                    for j in range(NT8):
                        ncol = g8 * cfg.G8 + j * 128
                        s.dma_start(wdt[idx % 2][j][:],
                                    dbf[hh * HH:(hh + 1) * HH, ncol:ncol + 128],
                                    transpose=True).then_inc(s_wt3[idx % 2], 16)
                s.wait_ge(s_dh, 16 * TT)
                for i in range(TT):
                    if i >= 2:
                        s.wait_ge(s_oc, i - 1)
                    s.dma_start(dfull[i % 2][:, 0:HH], dhalf[i * 128:(i + 1) * 128, :]
                                ).then_inc(s_fd[i % 2], 16)
                    if i >= 1:
                        s.wait_ge(s_oc, i)
                    s.dma_start(x2l[:], x2d[i * 128:(i + 1) * 128, :]
                                ).then_inc(s_fx, 16)

            @blk.gpsimd
            def _(g):
                for idx in range(NIDX):
                    hh, g8 = divmod(idx, NG8)
                    if idx >= 2:
                        g.wait_ge(s_dm, TT * (idx - 1))
                    for j in range(NT8):
                        nrow = g8 * cfg.G8 + j * 128
                        g.dma_start(ig[idx % 2][j][:], Id[nrow:nrow + 128, :]
                                    ).then_inc(s_il[idx % 2], 16)
                    if idx == NG8 - 1:
                        g.wait_ge(s_da, TT * NG8)
                        for i in range(TT):
                            g.dma_start(dhalf[i * 128:(i + 1) * 128, :], D_sb[i][:]
                                        ).then_inc(s_dh, 16)
                for i in range(TT):
                    g.wait_ge(s_oc, i + 1)
                    g.dma_start(out[i * 128:(i + 1) * 128, :], obuf[:]
                                ).then_inc(s_out, 16)
                g.wait_ge(s_out, 16 * TT)

            @blk.tensor
            def _(t):
                for idx in range(NIDX):
                    t.wait_ge(s_wt3[idx % 2], 16 * NT8 * (idx // 2 + 1))
                    t.wait_ge(s_il[idx % 2], 16 * NT8 * (idx // 2 + 1))
                    for i in range(TT):
                        gidx = idx * TT + i
                        if gidx >= 3:
                            t.wait_ge(s_da, gidx - 2)
                        mm = None
                        for j in range(NT8):
                            for hc in range(HC):
                                mm = t.matmul(ps_d[gidx % 3][:, hc * HCW:(hc + 1) * HCW],
                                              ig[idx % 2][j][:, i * 128:(i + 1) * 128],
                                              wdt[idx % 2][j][:, hc * HCW:(hc + 1) * HCW],
                                              start=(j == 0), stop=(j == NT8 - 1))
                        mm.then_inc(s_dm, 1)

            @blk.scalar
            def _(a):
                load_act_set(nc, a, SET_LN)
                for i in range(TT):
                    col = 3 * TT + i
                    a.wait_ge(s_da, GF + i + 1)
                    a.wait_ge(s_fd[i % 2], 16 * (i // 2 + 1))
                    if i >= 2:
                        a.wait_ge(s_oc, i - 1)
                    a.activation(dfull[i % 2][:, HH:C], D_sb[i][:], AF.Copy
                                 ).then_inc(s_df, 1)
                    ln_a_chain(a, col, dfull[i % 2][:, :], scr8[:])

            @blk.vector
            def _(v):
                for idx in range(NIDX):
                    hh, g8 = divmod(idx, NG8)
                    for i in range(TT):
                        gidx = idx * TT + i
                        v.wait_ge(s_dm, gidx + 1)
                        if hh == 1 and g8 == 0 and i == 0:
                            v.wait_ge(s_dh, 16 * TT)
                        if g8 == 0:
                            v.tensor_copy(D_sb[i][:], ps_d[gidx % 3][:]).then_inc(s_da, 1)
                        else:
                            v.tensor_tensor(D_sb[i][:], D_sb[i][:], ps_d[gidx % 3][:],
                                            ALU.add).then_inc(s_da, 1)
                        if idx == NIDX - 1:
                            def b9_stats(i):
                                col = 3 * TT + i
                                v.wait_ge(s_df, i + 1)
                                v.wait_ge(s_fd[i % 2], 16 * (i // 2 + 1))
                                if i >= 1:
                                    sP, bP = lnsem(col - 1)
                                    v.wait_ge(sP, bP + 2)
                                ln_v_stats(v, col, dfull[i % 2][:, :], C)

                            def b9_out(m):
                                col = 3 * TT + m
                                ln_v_nmr(v, col)
                                if m >= 1:
                                    v.wait_ge(s_out, 16 * m)
                                ln_v_norm(v, col, obuf[:], dfull[m % 2][:, :])
                                v.wait_ge(s_fx, 16 * (m + 1))
                                v.tensor_tensor(obuf[:], obuf[:],
                                                x2l[:], ALU.add).then_inc(s_oc, 1)

                            b9_stats(i)
                            if i >= 1:
                                b9_out(i - 1)
                            if i == TT - 1:
                                b9_out(i)

        b8.close()
        b8d.close()

    return nc


def _in_maps(cfg, x, W, b, gW, gS, uW, uS, dW, dS):
    maps = []
    for r in range(NCORES):
        maps.append({
            "x": np.ascontiguousarray(x[r]),
            "W_slT": np.ascontiguousarray(W[r * cfg.Ds:(r + 1) * cfg.Ds].T),
            "b": np.ascontiguousarray(b.reshape(1, -1)),
            "gW_sl": np.ascontiguousarray(gW[r * cfg.Ns:(r + 1) * cfg.Ns]),
            "gS_sl": np.ascontiguousarray(gS[r * cfg.Ns:(r + 1) * cfg.Ns]),
            "uW_sl": np.ascontiguousarray(uW[r * cfg.Ns:(r + 1) * cfg.Ns]),
            "uS_sl": np.ascontiguousarray(uS[r * cfg.Ns:(r + 1) * cfg.Ns]),
            "dW_sl": np.ascontiguousarray(dW[r * cfg.Cs:(r + 1) * cfg.Cs]),
            "dS_sl": np.ascontiguousarray(dS[r * cfg.Cs:(r + 1) * cfg.Cs]),
        })
    return maps


def kernel(L0_attn_x, L0_attn_W, L0_attn_b, L0_ffn_gate_W_packed, L0_ffn_gate_scales,
           L0_ffn_up_W_packed, L0_ffn_up_scales, L0_ffn_down_W_packed,
           L0_ffn_down_scales):
    x = np.asarray(L0_attn_x)
    B, T, C = x.shape
    N = L0_ffn_gate_W_packed.shape[0]
    cfg = Cfg(T=T, C=C, N=N)
    nc = build_program(cfg)
    maps = _in_maps(cfg, x, np.asarray(L0_attn_W), np.asarray(L0_attn_b),
                    np.asarray(L0_ffn_gate_W_packed), np.asarray(L0_ffn_gate_scales),
                    np.asarray(L0_ffn_up_W_packed), np.asarray(L0_ffn_up_scales),
                    np.asarray(L0_ffn_down_W_packed), np.asarray(L0_ffn_down_scales))
    res = run_bass_kernel_spmd(nc, maps, core_ids=list(range(NCORES)))
    return np.stack([res.results[r]["out"] for r in range(NCORES)], axis=0)


if __name__ == "__main__":
    pass


# version 28
# speedup vs baseline: 1.0221x; 1.0034x over previous
import contextlib
import numpy as np
import concourse.bass as bass
import concourse.mybir as mybir
from concourse.bass_utils import run_bass_kernel_spmd

F32 = mybir.dt.float32
BF16 = mybir.dt.bfloat16
I32 = mybir.dt.int32
U8 = mybir.dt.uint8
AF = mybir.ActivationFunctionType
ALU = mybir.AluOpType
AX = mybir.AxisListType

NCORES = 8
SET_LN = 6
SET_SIG = 2
SET_GELU = 11
LNP = 16


def load_act_set(nc, eng, idx):
    eng.add_instruction(mybir.InstLoadActFuncSet(
        name=nc.get_next_instruction_name(), act_func_set_id=idx, ins=[], outs=[]))


class Cfg:
    def __init__(self, T, C, N):
        assert T % 128 == 0 and C % 256 == 0 and N % 1024 == 0
        self.T, self.C, self.N = T, C, N
        self.D = 2 * C
        self.TT = T // 128
        self.CT = C // 128
        self.NT = N // 128
        self.TCW = min(T, 512)
        self.TC = T // self.TCW
        self.JU = C // 2
        self.BU = C // 32
        self.Ds = self.D // NCORES
        self.Ns = self.N // NCORES
        self.Cs = self.C // NCORES
        self.NG7 = N // 512
        self.HH = C // 2
        self.HCW = min(self.HH, 512)
        self.HC = self.HH // self.HCW
        self.G8 = min(N, 1024)
        self.NG8 = N // self.G8
        self.NT8 = self.G8 // 128


def build_program(cfg, stop_after=None):
    T, C, D, N = cfg.T, cfg.C, cfg.D, cfg.N
    TT, CT, NT = cfg.TT, cfg.CT, cfg.NT
    TC, TCW = cfg.TC, cfg.TCW
    HH, HC, HCW = cfg.HH, cfg.HC, cfg.HCW
    JU, BU = cfg.JU, cfg.BU
    nc = bass.Bass("TRN2", target_bir_lowering=False, detect_race_conditions=False)

    x_in = nc.dram_tensor("x", [T, C], F32, kind="ExternalInput")
    W_slT = nc.dram_tensor("W_slT", [C, cfg.Ds], F32, kind="ExternalInput")
    b_in = nc.dram_tensor("b", [128, D // 128], F32, kind="ExternalInput")
    gW_sl = nc.dram_tensor("gW_sl", [cfg.Ns, C // 2], I32, kind="ExternalInput")
    gS_sl = nc.dram_tensor("gS_sl", [cfg.Ns, C // 32], F32, kind="ExternalInput")
    uW_sl = nc.dram_tensor("uW_sl", [cfg.Ns, C // 2], I32, kind="ExternalInput")
    uS_sl = nc.dram_tensor("uS_sl", [cfg.Ns, C // 32], F32, kind="ExternalInput")
    dW_sl = nc.dram_tensor("dW_sl", [cfg.Cs, N // 2], I32, kind="ExternalInput")
    dS_sl = nc.dram_tensor("dS_sl", [cfg.Cs, N // 32], F32, kind="ExternalInput")
    out = nc.dram_tensor("out", [T, C], F32, kind="ExternalOutput")

    Wbf_in = nc.dram_tensor("Wbf_in", [C, cfg.Ds], BF16)
    Wbf = nc.dram_tensor("Wbf", [NCORES * C, cfg.Ds], BF16)
    gbf_in = nc.dram_tensor("gbf_in", [cfg.Ns, C], BF16)
    NCH = max(1, (N // NCORES) // 512)
    RPC = (N // NCORES) // NCH
    gbfc = [nc.dram_tensor(f"gbf{c}", [RPC * NCORES, C], BF16) for c in range(NCH)]
    ubf_in = nc.dram_tensor("ubf_in", [cfg.Ns, C], BF16)
    ubfc = [nc.dram_tensor(f"ubf{c}", [RPC * NCORES, C], BF16) for c in range(NCH)]
    dbf_in = nc.dram_tensor("dbf_in", [cfg.Cs, N], BF16)
    dbf = nc.dram_tensor("dbf", [C, N], BF16)
    ln1d = nc.dram_tensor("ln1d", [T, C], BF16)
    hhi = nc.dram_tensor("hhi", [C, T], BF16)
    hlo = nc.dram_tensor("hlo", [C, T], BF16)
    x2d = nc.dram_tensor("x2d", [T, C], F32)
    Id = nc.dram_tensor("Id", [N, T], BF16)
    dhalf = nc.dram_tensor("dhalf", [T, HH], F32)

    top = contextlib.ExitStack()
    sem = lambda name: top.enter_context(nc.semaphore(name))
    sbT = lambda st, name, shape, dt: st.enter_context(nc.sbuf_tensor(name, shape, dt))
    psT = lambda st, name, shape: st.enter_context(nc.psum_tensor(name, shape, F32))

    with top:
        c_m1 = sbT(top, "c_m1", [128, 1], F32)
        c_75 = sbT(top, "c_75", [128, 1], F32)
        c_eps = sbT(top, "c_eps", [128, 1], F32)
        c_mh = sbT(top, "c_mh", [128, 1], F32)
        c_zero = sbT(top, "c_zero", [128, 1], F32)
        c_one = sbT(top, "c_one", [128, 1], F32)
        bsh = sbT(top, "bsh", [128, D // 128], F32)
        NCOL = 4 * TT
        st_sum = sbT(top, "st_sum", [128, NCOL], F32)
        st_sq = sbT(top, "st_sq", [128, NCOL], F32)
        st_m = sbT(top, "st_m", [128, NCOL], F32)
        st_v0 = sbT(top, "st_v0", [128, NCOL], F32)
        st_nv = sbT(top, "st_nv", [128, NCOL], F32)
        st_ln = sbT(top, "st_ln", [128, NCOL], F32)
        st_r = sbT(top, "st_r", [128, NCOL], F32)
        st_nmr = sbT(top, "st_nmr", [128, NCOL], F32)
        idm = sbT(top, "idm", [128, 128], BF16)

        ln_sems = [sem(f"s_ln{i}") for i in range(LNP)]

        def lnsem(col):
            return ln_sems[col % LNP], 8 * (col // LNP)

        def ln_v_stats(v, col, src_ap, F):
            s, base = lnsem(col)
            cs = slice(col, col + 1)
            v.reduce_sum(st_sum[:, cs], src_ap, AX.X).then_inc(s, 1)
            v.wait_ge(s, base + 2)
            v.tensor_scalar(st_m[:, cs], st_sum[:, cs], 1.0 / F, None, ALU.mult).then_inc(s, 1)
            v.tensor_scalar(st_v0[:, cs], st_sq[:, cs], 1.0 / F, None, ALU.mult).then_inc(s, 1)
            v.wait_ge(s, base + 4)
            v.tensor_scalar(st_nv[:, cs], st_m[:, cs], st_m[:, cs], st_v0[:, cs],
                            ALU.mult, ALU.subtract).then_inc(s, 1)

        def ln_v_nmr(v, col):
            s, base = lnsem(col)
            cs = slice(col, col + 1)
            v.wait_ge(s, base + 7)
            v.scalar_tensor_tensor(st_nmr[:, cs], st_m[:, cs], -1.0, st_r[:, cs],
                                   ALU.mult, ALU.mult).then_inc(s, 1)

        def ln_v_norm(v, col, out_ap, src_ap):
            s, base = lnsem(col)
            cs = slice(col, col + 1)
            v.wait_ge(s, base + 8)
            return v.tensor_scalar(out_ap, src_ap, st_r[:, cs], st_nmr[:, cs],
                                   ALU.mult, ALU.add)

        def ln_a_chain(a, col, src_ap, scr_ap):
            s, base = lnsem(col)
            cs = slice(col, col + 1)
            a.activation(scr_ap, src_ap, AF.Square, accum_out=st_sq[:, cs]).then_inc(s, 1)
            a.wait_ge(s, base + 5)
            a.activation(st_ln[:, cs], st_nv[:, cs], AF.Ln,
                         bias=c_eps[:], scale=c_m1[:]).then_inc(s, 1)
            a.wait_ge(s, base + 6)
            a.activation(st_r[:, cs], st_ln[:, cs], AF.Exp,
                         bias=c_zero[:], scale=c_mh[:]).then_inc(s, 1)

        units = []
        for nt in range(cfg.Ns // 128):
            units.append((gW_sl, gS_sl, gbf_in, nt, 0))
        NU_G = len(units)
        for nt in range(cfg.Ns // 128):
            units.append((uW_sl, uS_sl, ubf_in, nt, 0))
        NU_U = len(units)
        for nt in range(cfg.Cs // 128):
            for jc in range((N // 2) // JU):
                units.append((dW_sl, dS_sl, dbf_in, nt, jc))
        NU = len(units)

        bi = contextlib.ExitStack()
        p8 = [sbT(bi, f"p8_{i}", [128, JU], U8) for i in range(2)]
        scs = sbT(bi, "scs", [128, NU * BU], F32)
        c8 = sbT(bi, "c8", [128, C], U8)
        cf = sbT(bi, "cf", [128, C], BF16)
        sgn = sbT(bi, "sgn", [128, C], BF16)
        dxv = sbT(bi, "dxv", [128, C], BF16)
        dt3 = sbT(bi, "dt3", [128, C], BF16)
        wout = [sbT(bi, f"wout_{i}", [128, C], BF16) for i in range(2)]
        itf = sbT(bi, "itf", [128, 128], I32)
        itp = sbT(bi, "itp", [128, 128], I32)
        xbuf = [sbT(bi, f"xbuf_{i}", [128, C], F32) for i in range(2)]
        lnbuf = [sbT(bi, f"lnbuf_{i}", [128, C], BF16) for i in range(2)]
        wtg = [sbT(bi, f"wtg_{p}", [128, CT * 256], BF16) for p in range(2)]
        wtv = [sbT(bi, f"wtv_{p}", [128, CT * 256], BF16) for p in range(2)]
        gsb = sbT(bi, "gsb", [128, T], F32)
        vsb = sbT(bi, "vsb", [128, T], F32)
        hsb = sbT(bi, "hsb", [128, T], F32)
        hib = sbT(bi, "hib", [128, T], BF16)
        hlb = sbT(bi, "hlb", [128, T], BF16)
        pT = contextlib.ExitStack()
        actT = pT.enter_context(nc.sbuf_tensor("actT", [128, CT * T], BF16, side="right"))
        psA = contextlib.ExitStack()
        PW = max(T, C)
        ps_g = psT(psA, "ps_g", [128, PW])
        ps_v = psT(psA, "ps_v", [128, PW])

        s_prep = sem("s_prep")
        s_ccw = sem("s_ccw")
        s_ccg = [sem(f"s_ccg{c}") for c in range(NCH)]
        s_ccu = [sem(f"s_ccu{c}") for c in range(NCH)]
        s_ccd = sem("s_ccd")
        s_pk = [sem("s_pk0"), sem("s_pk1")]
        s_sc = sem("s_sc")
        s_nib = sem("s_nib")
        s_dec = sem("s_dec")
        s_wo = sem("s_wo")
        s_xl = [sem("s_xl0"), sem("s_xl1")]
        s_n1 = sem("s_n1")
        s_o1 = sem("s_o1")
        s_t1 = sem("s_t1")
        s_wt = [sem("s_wt0"), sem("s_wt1")]
        s_mm = sem("s_mm")
        s_ev = sem("s_ev")
        s_hv = sem("s_hv")
        s_hs = sem("s_hs")
        s_gm = sem("s_gm")
        s_im = sem("s_im")
        RG = [list(range(NCORES))]

        with nc.Block() as blk:
            @blk.gpsimd
            def _(g):
                g.iota(itf[:], [[1, 128]], base=0, channel_multiplier=0).then_inc(s_im, 1)
                g.iota(itp[:], [[0, 128]], base=0, channel_multiplier=1).then_inc(s_im, 1)
                g.memset(c_m1[:], -1.0)
                g.memset(c_75[:], 7.5)
                g.memset(c_eps[:], 1e-5)
                g.memset(c_mh[:], -0.5)
                g.memset(c_zero[:], 0.0)
                g.memset(c_one[:], 1.0)
                g.dma_start(Wbf_in[:, :], W_slT[:, :]).then_inc(s_prep, 16)
                g.dma_start(bsh[:, :], b_in[:, :]).then_inc(s_prep, 16)
                g.wait_ge(s_prep, 32)
                g.collective_compute("AllGather", ALU.bypass, replica_groups=RG,
                                     ins=[Wbf_in.ap().opt()], outs=[Wbf.ap().opt()]
                                     ).then_inc(s_ccw, 1)

                def emit_out(i):
                    _, _, dst, nt, jc = units[i]
                    g.wait_ge(s_dec, i + 1)
                    g.dma_start(dst[nt * 128:(nt + 1) * 128, jc * C:(jc + 1) * C],
                                wout[i % 2][:]).then_inc(s_wo, 16)


                for i, (src, ssrc, dst, nt, jc) in enumerate(units):
                    if i >= 2:
                        g.wait_ge(s_nib, i - 1)
                    g.dma_start(p8[i % 2][:],
                                src[nt * 128:(nt + 1) * 128, jc * JU:(jc + 1) * JU]
                                ).then_inc(s_pk[i % 2], 16)
                    if i >= 1:
                        emit_out(i - 1)
                emit_out(NU - 1)
                g.wait_ge(s_wo, 16 * NU)
                g.wait_ge(s_hs, 32 * CT)
                g.collective_compute("AllGather", ALU.bypass, replica_groups=RG,
                                     ins=[gbf_in[0:RPC, :].opt()],
                                     outs=[gbfc[0].ap().opt()]).then_inc(s_ccg[0], 1)
                g.collective_compute("AllGather", ALU.bypass, replica_groups=RG,
                                     ins=[ubf_in[0:RPC, :].opt()],
                                     outs=[ubfc[0].ap().opt()]).then_inc(s_ccu[0], 1)

            @blk.sync
            def _(s):
                for i in range(TT):
                    if i >= 2:
                        s.wait_ge(s_n1, i - 1)
                    s.dma_start(xbuf[i % 2][:], x_in[i * 128:(i + 1) * 128, :]
                                ).then_inc(s_xl[i % 2], 16)
                for i, (src, ssrc, dst, nt, jc) in enumerate(units):
                    s.dma_start(scs[:, i * BU:(i + 1) * BU],
                                ssrc[nt * 128:(nt + 1) * 128, jc * BU:(jc + 1) * BU]
                                ).then_inc(s_sc, 16)
                s.wait_ge(s_ccw, 1)

                def wt_sp(q):
                    if q >= 2:
                        s.wait_ge(s_mm, 4 * (q - 1))
                    fg = q * 256
                    rg, cg = fg // cfg.Ds, fg % cfg.Ds
                    fv = C + q * 256
                    rv, cv_ = fv // cfg.Ds, fv % cfg.Ds
                    for j in range(CT):
                        s.dma_start(wtg[q % 2][:, j * 256:(j + 1) * 256],
                                    Wbf[rg * C + j * 128:rg * C + (j + 1) * 128,
                                        cg:cg + 256]).then_inc(s_wt[q % 2], 16)
                        s.dma_start(wtv[q % 2][:, j * 256:(j + 1) * 256],
                                    Wbf[rv * C + j * 128:rv * C + (j + 1) * 128,
                                        cv_:cv_ + 256]).then_inc(s_wt[q % 2], 16)

                NQ = CT // 2
                wt_sp(0)
                wt_sp(1)
                s.wait_ge(s_o1, 16 * TT)
                for j in range(CT):
                    s.dma_start(actT[:, j * T:(j + 1) * T],
                                ln1d[:, j * 128:(j + 1) * 128], transpose=True
                                ).then_inc(s_t1, 16)
                for q in range(2, NQ):
                    wt_sp(q)

            @blk.vector
            def _(v):
                v.wait_ge(s_im, 2)
                v.tensor_tensor(idm[:], itf[:], itp[:], ALU.is_equal).then_inc(s_im, 1)

                def a1_tile(i):
                    v.wait_ge(s_xl[i % 2], 16 * (i // 2 + 1))
                    ln_v_stats(v, i, xbuf[i % 2][:, :], C)
                    ln_v_nmr(v, i)
                    if i >= 2:
                        v.wait_ge(s_o1, 16 * i)
                    ln_v_norm(v, i, lnbuf[i % 2][:], xbuf[i % 2][:]).then_inc(s_n1, 1)

                def dec_unit(i):
                    pb = p8[i % 2]
                    v.wait_ge(s_pk[i % 2], 16 * (i // 2 + 1))
                    cv = c8.ap().rearrange("a (j two) -> a j two", two=2)
                    v.tensor_scalar(cv[:, :, 0:1], pb.ap().unsqueeze(2), 15, None,
                                    ALU.bitwise_and)
                    v.tensor_scalar(cv[:, :, 1:2], pb.ap().unsqueeze(2), 4, None,
                                    ALU.logical_shift_right)
                    v.tensor_copy(cf[:], c8[:]).then_inc(s_nib, 1)
                    sg = sgn
                    v.tensor_scalar(sg[:], cf[:], -2.0, 15.0, ALU.mult, ALU.add)
                    v.tensor_scalar(sg[:], sg[:], -1.0, None, ALU.max)
                    v.tensor_scalar(sg[:], sg[:], 1.0, None, ALU.min)
                    v.scalar_tensor_tensor(dxv[:], sg[:], 4.0, cf[:], ALU.mult, ALU.add)
                    v.tensor_scalar(cf[:], dxv[:], 0.5, -2.0, ALU.mult, ALU.add)
                    v.scalar_tensor_tensor(cf[:], dxv[:], 6.0, cf[:], ALU.subtract, ALU.max)
                    v.tensor_scalar(dt3[:], dxv[:], 2.0, -16.0, ALU.mult, ALU.add)
                    v.tensor_tensor(dt3[:], cf[:], dt3[:], ALU.max)
                    v.tensor_tensor(dxv[:], dt3[:], sg[:], ALU.mult)
                    if i >= 2:
                        v.wait_ge(s_wo, 16 * i)
                    if i == 0:
                        v.wait_ge(s_sc, 16 * NU)
                    vv = dxv.ap().rearrange("a (b k) -> a b k", k=32)
                    wv = wout[i % 2].ap().rearrange("a (b k) -> a b k", k=32)
                    scb = scs[:, i * BU:(i + 1) * BU].unsqueeze(2).broadcast_to([128, BU, 32])
                    v.tensor_tensor(wv, vv, scb, ALU.mult).then_inc(s_dec, 1)

                def scan_pair(k):
                    v.wait_ge(s_ev, 2 * (k + 1))
                    if k >= 1:
                        v.wait_ge(s_hs, 32 * k)
                    v.tensor_tensor_scan(hsb[:], gsb[:], vsb[:], 0.0, ALU.mult, ALU.add)
                    v.tensor_copy(hib[:], hsb[:])
                    v.tensor_tensor(hlb[:], hsb[:], hib[:], ALU.subtract).then_inc(s_hv, 1)

                for i in range(TT):
                    a1_tile(i)
                ui = 0

                def take(n):
                    nonlocal ui
                    stop = min(ui + n, NU)
                    while ui < stop:
                        dec_unit(ui)
                        ui += 1

                take(2)
                for k in range(CT):
                    scan_pair(k)
                    take(3)
                take(NU)

            @blk.scalar
            def _(a):
                load_act_set(nc, a, SET_LN)
                for i in range(TT):
                    a.wait_ge(s_xl[i % 2], 16 * (i // 2 + 1))
                    ln_a_chain(a, i, xbuf[i % 2][:, :], ps_g[:, 0:C])
                    a.wait_ge(s_n1, i + 1)
                    nc.scalar.dma_start(ln1d[i * 128:(i + 1) * 128, :], lnbuf[i % 2][:]
                                        ).then_inc(s_o1, 16)
                load_act_set(nc, a, SET_SIG)
                a.wait_ge(s_prep, 32)
                for k in range(CT):
                    if k >= 1:
                        a.wait_ge(s_hv, k)
                        nc.scalar.dma_start(hhi[(k - 1) * 128:k * 128, :], hib[:]
                                            ).then_inc(s_hs, 16)
                        nc.scalar.dma_start(hlo[(k - 1) * 128:k * 128, :], hlb[:]
                                            ).then_inc(s_hs, 16)
                    a.wait_ge(s_mm, 2 * k + 1)
                    if k >= 1:
                        a.wait_ge(s_hv, k)
                    a.activation(gsb[:], ps_g[:, 0:T], AF.Sigmoid,
                                 bias=bsh[:, k:k + 1], scale=c_one[:]).then_inc(s_ev, 1)
                    a.wait_ge(s_mm, 2 * k + 2)
                    a.activation(vsb[:], ps_v[:, 0:T], AF.Tanh,
                                 bias=bsh[:, CT + k:CT + k + 1], scale=c_one[:]
                                 ).then_inc(s_ev, 1)
                a.wait_ge(s_hv, CT)
                nc.scalar.dma_start(hhi[(CT - 1) * 128:CT * 128, :], hib[:]).then_inc(s_hs, 16)
                nc.scalar.dma_start(hlo[(CT - 1) * 128:CT * 128, :], hlb[:]).then_inc(s_hs, 16)

            @blk.tensor
            def _(t):
                t.wait_ge(s_t1, 16 * CT)
                for k in range(CT):
                    q, kq = divmod(k, 2)
                    if kq == 0:
                        t.wait_ge(s_wt[q % 2], 32 * CT * (q // 2 + 1))
                    for psum, wtiles, boff in ((ps_g, wtg[q % 2], 0), (ps_v, wtv[q % 2], C)):
                        if k >= 1:
                            t.wait_ge(s_ev, 2 * k - (1 if boff == 0 else 0))
                        mm = None
                        for j in range(CT):
                            for tc in range(TC):
                                mm = t.matmul(psum[:, tc * TCW:(tc + 1) * TCW],
                                              wtiles[:, j * 256 + kq * 128:
                                                     j * 256 + (kq + 1) * 128],
                                              actT[:, j * T + tc * TCW: j * T + (tc + 1) * TCW],
                                              start=(j == 0), stop=(j == CT - 1))
                        mm.then_inc(s_mm, 1)

        bi.close()
        psA.close()

        a4 = contextlib.ExitStack()
        psB = contextlib.ExitStack()
        ps2 = [psB.enter_context(nc.psum_tensor(f"ps2_{p}", [128, C], BF16))
               for p in range(2)]
        ps_h = psB.enter_context(nc.psum_tensor("ps_h", [128, C], BF16))
        ps_l = psB.enter_context(nc.psum_tensor("ps_l", [128, C], BF16))
        xbuf2 = [sbT(a4, f"xbuf2_{p}", [128, C], F32) for p in range(4)]
        lnbuf2 = [sbT(a4, f"lnbuf2_{p}", [128, C], BF16) for p in range(2)]
        hif = [sbT(a4, f"hif_{p}", [128, C], BF16) for p in range(2)]
        lof = [sbT(a4, f"lof_{p}", [128, C], BF16) for p in range(2)]
        hrow = [sbT(a4, f"hrow_{p}", [128, C], F32) for p in range(2)]
        lsb = sbT(a4, "lsb", [128, C], BF16)
        alnb = sbT(a4, "alnb", [128, C], F32)
        x2b = [sbT(a4, f"x2b_{p}", [128, C], F32) for p in range(2)]
        scr4 = sbT(a4, "scr4", [128, C], F32)
        s_hf = [sem("s_hf0"), sem("s_hf1")]
        s_xl2 = [sem(f"s_xl2{p}") for p in range(4)]
        s_thm = sem("s_thm")
        s_hr = sem("s_hr")
        s_n2 = sem("s_n2")
        s_o2x = [sem("s_o2x0"), sem("s_o2x1")]
        s_tm = sem("s_tm")
        s_tc = sem("s_tc")

        hhi_v = hhi.ap().rearrange("(k p) t -> p k t", p=128)
        hlo_v = hlo.ap().rearrange("(k p) t -> p k t", p=128)

        with nc.Block() as blk:
            @blk.sync
            def _(s):
                s.wait_ge(s_hs, 32 * CT)
                for i in range(TT):
                    if i >= 2:
                        s.wait_ge(s_thm, i - 1)
                    if i >= 4:
                        s.wait_ge(s_n2, 2 * (i - 4) + 1)
                    dst = hif[i % 2].ap().rearrange("p (k t) -> p k t", t=128)
                    s.dma_start(dst, hhi_v[:, :, i * 128:(i + 1) * 128]
                                ).then_inc(s_hf[i % 2], 16)
                    dst = lof[i % 2].ap().rearrange("p (k t) -> p k t", t=128)
                    s.dma_start(dst, hlo_v[:, :, i * 128:(i + 1) * 128]
                                ).then_inc(s_hf[i % 2], 16)
                    s.dma_start(xbuf2[i % 4][:], x_in[i * 128:(i + 1) * 128, :]
                                ).then_inc(s_xl2[i % 4], 16)

            @blk.tensor
            def _(t):
                t.wait_ge(s_im, 3)
                for i in range(TT + 2):
                    if i < TT:
                        t.wait_ge(s_hf[i % 2], 32 * (i // 2 + 1))
                        if i >= 1:
                            t.wait_ge(s_hr, i)
                        mm = None
                        for k in range(CT):
                            mm = t.transpose(ps_h[:, k * 128:(k + 1) * 128],
                                             hif[i % 2][:, k * 128:(k + 1) * 128],
                                             idm[:])
                        for k in range(CT):
                            mm = t.transpose(ps_l[:, k * 128:(k + 1) * 128],
                                             lof[i % 2][:, k * 128:(k + 1) * 128],
                                             idm[:])
                        mm.then_inc(s_thm, 1)
                    if i >= 2:
                        m = i - 2
                        t.wait_ge(s_n2, 2 * m + 2)
                        if m >= 2:
                            t.wait_ge(s_tc, CT * (m - 1))
                        mm = None
                        for j in range(CT):
                            mm = t.transpose(ps2[m % 2][:, j * 128:(j + 1) * 128],
                                             lnbuf2[m % 2][:, j * 128:(j + 1) * 128],
                                             idm[:])
                        mm.then_inc(s_tm, 1)

            @blk.scalar
            def _(a):
                load_act_set(nc, a, SET_LN)
                for i in range(TT):
                    ca, cb_ = TT + 2 * i, TT + 2 * i + 1
                    a.wait_ge(s_hr, i + 1)
                    ln_a_chain(a, ca, hrow[i % 2][:, :], scr4[:])
                    a.wait_ge(s_n2, 2 * i + 1)
                    ln_a_chain(a, cb_, x2b[i % 2][:, :], scr4[:])
                    if i >= 2:
                        a.wait_ge(s_tm, i - 1)
                        m = i - 2
                        for j in range(CT):
                            a.activation(actT[:, j * T + m * 128:j * T + (m + 1) * 128],
                                         ps2[m % 2][:, j * 128:(j + 1) * 128],
                                         AF.Copy).then_inc(s_tc, 1)
                for m in (TT - 2, TT - 1):
                    a.wait_ge(s_tm, m + 1)
                    for j in range(CT):
                        a.activation(actT[:, j * T + m * 128:j * T + (m + 1) * 128],
                                     ps2[m % 2][:, j * 128:(j + 1) * 128],
                                     AF.Copy).then_inc(s_tc, 1)

            @blk.vector
            def _(v):
                for i in range(TT):
                    ca, cb_ = TT + 2 * i, TT + 2 * i + 1
                    v.wait_ge(s_thm, i + 1)
                    v.tensor_copy(lsb[:], ps_l[:])
                    v.tensor_tensor(hrow[i % 2][:], ps_h[:], lsb[:],
                                    ALU.add).then_inc(s_hr, 1)
                    ln_v_stats(v, ca, hrow[i % 2][:, :], C)
                    ln_v_nmr(v, ca)
                    sA, baseA = lnsem(ca)
                    v.wait_ge(sA, baseA + 8)
                    v.tensor_scalar(alnb[:], hrow[i % 2][:], st_r[:, ca:ca + 1],
                                    st_nmr[:, ca:ca + 1], ALU.mult, ALU.add)
                    v.wait_ge(s_xl2[i % 4], 16 * (i // 4 + 1))
                    if i >= 2:
                        v.wait_ge(s_o2x[i % 2], 16 * (i // 2))
                    v.tensor_tensor(x2b[i % 2][:], alnb[:], xbuf2[i % 4][:],
                                    ALU.add).then_inc(s_n2, 1)
                    ln_v_stats(v, cb_, x2b[i % 2][:, :], C)
                    ln_v_nmr(v, cb_)
                    if i >= 2:
                        v.wait_ge(s_tm, i - 1)
                    ln_v_norm(v, cb_, lnbuf2[i % 2][:], x2b[i % 2][:]).then_inc(s_n2, 1)

            @blk.gpsimd
            def _(g):
                for i in range(TT):
                    g.wait_ge(s_n2, 2 * i + 1)
                    g.dma_start(x2d[i * 128:(i + 1) * 128, :], x2b[i % 2][:]).then_inc(s_o2x[i % 2], 16)

        a4.close()
        psB.close()

        if stop_after == "BI":
            return nc

        b7 = contextlib.ExitStack()
        wgt = [[sbT(b7, f"wgt_{p}_{j}", [128, 512], BF16) for j in range(CT)] for p in range(3)]
        wut = [[sbT(b7, f"wut_{p}_{j}", [128, 512], BF16) for j in range(CT)] for p in range(3)]
        glb = [sbT(b7, f"glb_{p}", [128, T], F32) for p in range(2)]
        ib = [sbT(b7, f"ib_{p}", [128, T], BF16) for p in range(2)]
        ps_G = psT(b7, "ps_G", [128, T])
        ps_U = psT(b7, "ps_U", [128, T])
        s_wt2 = [sem("s_wt20"), sem("s_wt21"), sem("s_wt22")]
        s_wu2 = [sem("s_wu20"), sem("s_wu21"), sem("s_wu22")]
        s_ge = sem("s_ge")
        s_iv = sem("s_iv")
        s_io = sem("s_io")
        NG7 = cfg.NG7

        g7order = [(c, r) for c in range(NCH) for r in range(NCORES)]
        assert len(g7order) * 512 == N
        id_base = [r * cfg.Ns + c * RPC for (c, r) in g7order]

        with nc.Block() as blk:
            @blk.sync
            def _(s):
                for g7, (c, r) in enumerate(g7order):
                    if r == 0:
                        s.wait_ge(s_ccg[c], 1)
                        s.wait_ge(s_ccu[c], 1)
                    if g7 >= 3:
                        s.wait_ge(s_gm, 8 * TC * (g7 - 2))
                    r0 = r * RPC
                    for j in range(CT):
                        s.dma_start(wgt[g7 % 3][j][:],
                                    gbfc[c][r0:r0 + 512, j * 128:(j + 1) * 128],
                                    transpose=True).then_inc(s_wt2[g7 % 3], 16)
                    for j in range(CT):
                        s.dma_start(wut[g7 % 3][j][:],
                                    ubfc[c][r0:r0 + 512, j * 128:(j + 1) * 128],
                                    transpose=True).then_inc(s_wu2[g7 % 3], 16)

            @blk.tensor
            def _(t):
                t.wait_ge(s_tc, CT * TT)
                for g7 in range(NG7):
                    t.wait_ge(s_wt2[g7 % 3], 16 * CT * (g7 // 3 + 1))
                    for nt in range(4):
                        nti = 4 * g7 + nt
                        mm = None
                        for tc in range(TC):
                            if nti >= 1:
                                t.wait_ge(s_ge, TC * (nti - 1) + tc + 1)
                            for j in range(CT):
                                mm = t.matmul(ps_G[:, tc * TCW:(tc + 1) * TCW],
                                              wgt[g7 % 3][j][:, nt * 128:(nt + 1) * 128],
                                              actT[:, j * T + tc * TCW: j * T + (tc + 1) * TCW],
                                              start=(j == 0), stop=(j == CT - 1))
                            mm.then_inc(s_gm, 1)
                        if nt == 0:
                            t.wait_ge(s_wu2[g7 % 3], 16 * CT * (g7 // 3 + 1))
                        for tc in range(TC):
                            if nti >= 1:
                                t.wait_ge(s_iv, TC * (nti - 1) + tc + 1)
                            for j in range(CT):
                                mm = t.matmul(ps_U[:, tc * TCW:(tc + 1) * TCW],
                                              wut[g7 % 3][j][:, nt * 128:(nt + 1) * 128],
                                              actT[:, j * T + tc * TCW: j * T + (tc + 1) * TCW],
                                              start=(j == 0), stop=(j == CT - 1))
                            mm.then_inc(s_gm, 1)

            @blk.scalar
            def _(a):
                load_act_set(nc, a, SET_GELU)
                for nti in range(NT):
                    for tc in range(TC):
                        a.wait_ge(s_gm, 2 * TC * nti + tc + 1)
                        if nti >= 2:
                            a.wait_ge(s_iv, TC * (nti - 2) + tc + 1)
                        a.activation(glb[nti % 2][:, tc * TCW:(tc + 1) * TCW],
                                     ps_G[:, tc * TCW:(tc + 1) * TCW],
                                     AF.Gelu_apprx_tanh).then_inc(s_ge, 1)

            @blk.vector
            def _(v):
                for nti in range(NT):
                    for tc in range(TC):
                        v.wait_ge(s_ge, TC * nti + tc + 1)
                        v.wait_ge(s_gm, 2 * TC * nti + TC + tc + 1)
                        if nti >= 2 and tc == 0:
                            v.wait_ge(s_io, 16 * nti)
                        v.tensor_tensor(ib[nti % 2][:, tc * TCW:(tc + 1) * TCW],
                                        glb[nti % 2][:, tc * TCW:(tc + 1) * TCW],
                                        ps_U[:, tc * TCW:(tc + 1) * TCW],
                                        ALU.mult).then_inc(s_iv, 1)

            @blk.gpsimd
            def _(g):
                GPC = (N // 512) // NCH
                ag_at = {4 * GPC * (c - 1) + 3: c for c in range(1, NCH)}
                d_at = 4 * GPC * min(2, NCH - 1) + 3
                for nti in range(NT):
                    row = id_base[nti // 4] + (nti % 4) * 128
                    g.wait_ge(s_iv, TC * (nti + 1))
                    g.dma_start(Id[row:row + 128, :], ib[nti % 2][:]
                                ).then_inc(s_io, 16)
                    if nti in ag_at:
                        c = ag_at[nti]
                        g.collective_compute("AllGather", ALU.bypass, replica_groups=RG,
                                             ins=[gbf_in[c * RPC:(c + 1) * RPC, :].opt()],
                                             outs=[gbfc[c].ap().opt()]).then_inc(s_ccg[c], 1)
                        g.collective_compute("AllGather", ALU.bypass, replica_groups=RG,
                                             ins=[ubf_in[c * RPC:(c + 1) * RPC, :].opt()],
                                             outs=[ubfc[c].ap().opt()]).then_inc(s_ccu[c], 1)
                    if nti == d_at:
                        g.collective_compute("AllGather", ALU.bypass, replica_groups=RG,
                                             ins=[dbf_in.ap().opt()], outs=[dbf.ap().opt()]
                                             ).then_inc(s_ccd, 1)
                g.wait_ge(s_io, 16 * NT)

        b7.close()
        pT.close()

        if stop_after == "B7":
            return nc

        b8 = contextlib.ExitStack()
        b8d = contextlib.ExitStack()
        NT8, NG8 = cfg.NT8, cfg.NG8
        D_sb = [sbT(b8d, f"D_sb_{i}", [128, HH], F32) for i in range(TT)]
        wdt = [[sbT(b8, f"wdt_{p}_{j}", [128, HH], BF16) for j in range(NT8)] for p in range(2)]
        ig = [[sbT(b8, f"ig_{p}_{j}", [128, T], BF16) for j in range(NT8)] for p in range(2)]
        dfull = [sbT(b8, f"dfull_{p}", [128, C], F32) for p in range(2)]
        x2l = sbT(b8, "x2l", [128, C], F32)
        obuf = sbT(b8, "obuf", [128, C], F32)
        ps_d = [psT(b8, f"ps_d_{q}", [128, HH]) for q in range(2)]
        ps_scr = psT(b8, "ps_scr", [128, C])
        s_wt3 = [sem("s_wt30"), sem("s_wt31")]
        s_il = [sem("s_il0"), sem("s_il1")]
        s_dm = sem("s_dm")
        s_da = sem("s_da")
        s_dh = sem("s_dh")
        s_fd = [sem("s_fd0"), sem("s_fd1")]
        s_fx = sem("s_fx")
        s_df = sem("s_df")
        s_oc = sem("s_oc")
        s_out = sem("s_out")
        NIDX = 2 * NG8
        GF = (NIDX - 1) * TT

        with nc.Block() as blk:
            @blk.sync
            def _(s):
                s.wait_ge(s_ccd, 1)
                for idx in range(NIDX):
                    hh, g8 = divmod(idx, NG8)
                    if idx >= 2:
                        s.wait_ge(s_dm, TT * (idx - 1))
                    for j in range(NT8):
                        ncol = g8 * cfg.G8 + j * 128
                        s.dma_start(wdt[idx % 2][j][:],
                                    dbf[hh * HH:(hh + 1) * HH, ncol:ncol + 128],
                                    transpose=True).then_inc(s_wt3[idx % 2], 16)
                s.wait_ge(s_dh, 16 * TT)
                for i in range(TT):
                    if i >= 2:
                        s.wait_ge(s_oc, i - 1)
                    s.dma_start(dfull[i % 2][:, 0:HH], dhalf[i * 128:(i + 1) * 128, :]
                                ).then_inc(s_fd[i % 2], 16)
                    if i >= 1:
                        s.wait_ge(s_oc, i)
                    s.dma_start(x2l[:], x2d[i * 128:(i + 1) * 128, :]
                                ).then_inc(s_fx, 16)

            @blk.gpsimd
            def _(g):
                for idx in range(NIDX):
                    hh, g8 = divmod(idx, NG8)
                    if idx >= 2:
                        g.wait_ge(s_dm, TT * (idx - 1))
                    for j in range(NT8):
                        nrow = g8 * cfg.G8 + j * 128
                        g.dma_start(ig[idx % 2][j][:], Id[nrow:nrow + 128, :]
                                    ).then_inc(s_il[idx % 2], 16)
                    if idx == NG8 - 1:
                        g.wait_ge(s_da, TT * NG8)
                        for i in range(TT):
                            g.dma_start(dhalf[i * 128:(i + 1) * 128, :], D_sb[i][:]
                                        ).then_inc(s_dh, 16)
                for i in range(TT):
                    g.wait_ge(s_oc, i + 1)
                    g.dma_start(out[i * 128:(i + 1) * 128, :], obuf[:]
                                ).then_inc(s_out, 16)
                g.wait_ge(s_out, 16 * TT)

            @blk.tensor
            def _(t):
                for idx in range(NIDX):
                    t.wait_ge(s_wt3[idx % 2], 16 * NT8 * (idx // 2 + 1))
                    t.wait_ge(s_il[idx % 2], 16 * NT8 * (idx // 2 + 1))
                    for i in range(TT):
                        gidx = idx * TT + i
                        if gidx >= 2:
                            t.wait_ge(s_da, gidx - 1)
                        mm = None
                        for j in range(NT8):
                            for hc in range(HC):
                                mm = t.matmul(ps_d[i % 2][:, hc * HCW:(hc + 1) * HCW],
                                              ig[idx % 2][j][:, i * 128:(i + 1) * 128],
                                              wdt[idx % 2][j][:, hc * HCW:(hc + 1) * HCW],
                                              start=(j == 0), stop=(j == NT8 - 1))
                        mm.then_inc(s_dm, 1)

            @blk.scalar
            def _(a):
                load_act_set(nc, a, SET_LN)
                for i in range(TT):
                    col = 3 * TT + i
                    a.wait_ge(s_da, GF + i + 1)
                    a.wait_ge(s_fd[i % 2], 16 * (i // 2 + 1))
                    if i >= 2:
                        a.wait_ge(s_oc, i - 1)
                    a.activation(dfull[i % 2][:, HH:C], D_sb[i][:], AF.Copy
                                 ).then_inc(s_df, 1)
                    ln_a_chain(a, col, dfull[i % 2][:, :], ps_scr[:])

            @blk.vector
            def _(v):
                for idx in range(NIDX):
                    hh, g8 = divmod(idx, NG8)
                    for i in range(TT):
                        gidx = idx * TT + i
                        v.wait_ge(s_dm, gidx + 1)
                        if hh == 1 and g8 == 0 and i == 0:
                            v.wait_ge(s_dh, 16 * TT)
                        if g8 == 0:
                            v.tensor_copy(D_sb[i][:], ps_d[i % 2][:]).then_inc(s_da, 1)
                        else:
                            v.tensor_tensor(D_sb[i][:], D_sb[i][:], ps_d[i % 2][:],
                                            ALU.add).then_inc(s_da, 1)
                        if idx == NIDX - 1:
                            def b9_stats(i):
                                col = 3 * TT + i
                                v.wait_ge(s_df, i + 1)
                                v.wait_ge(s_fd[i % 2], 16 * (i // 2 + 1))
                                if i >= 1:
                                    sP, bP = lnsem(col - 1)
                                    v.wait_ge(sP, bP + 2)
                                ln_v_stats(v, col, dfull[i % 2][:, :], C)

                            def b9_out(m):
                                col = 3 * TT + m
                                ln_v_nmr(v, col)
                                if m >= 1:
                                    v.wait_ge(s_out, 16 * m)
                                ln_v_norm(v, col, obuf[:], dfull[m % 2][:, :])
                                v.wait_ge(s_fx, 16 * (m + 1))
                                v.tensor_tensor(obuf[:], obuf[:],
                                                x2l[:], ALU.add).then_inc(s_oc, 1)

                            b9_stats(i)
                            if i >= 1:
                                b9_out(i - 1)
                            if i == TT - 1:
                                b9_out(i)

        b8.close()
        b8d.close()

    return nc


def _in_maps(cfg, x, W, b, gW, gS, uW, uS, dW, dS):
    maps = []
    for r in range(NCORES):
        maps.append({
            "x": np.ascontiguousarray(x[r]),
            "W_slT": np.ascontiguousarray(W[r * cfg.Ds:(r + 1) * cfg.Ds].T),
            "b": np.ascontiguousarray(b.reshape(-1, 128).T),
            "gW_sl": np.ascontiguousarray(gW[r * cfg.Ns:(r + 1) * cfg.Ns]),
            "gS_sl": np.ascontiguousarray(gS[r * cfg.Ns:(r + 1) * cfg.Ns]),
            "uW_sl": np.ascontiguousarray(uW[r * cfg.Ns:(r + 1) * cfg.Ns]),
            "uS_sl": np.ascontiguousarray(uS[r * cfg.Ns:(r + 1) * cfg.Ns]),
            "dW_sl": np.ascontiguousarray(dW[r * cfg.Cs:(r + 1) * cfg.Cs]),
            "dS_sl": np.ascontiguousarray(dS[r * cfg.Cs:(r + 1) * cfg.Cs]),
        })
    return maps


def kernel(L0_attn_x, L0_attn_W, L0_attn_b, L0_ffn_gate_W_packed, L0_ffn_gate_scales,
           L0_ffn_up_W_packed, L0_ffn_up_scales, L0_ffn_down_W_packed,
           L0_ffn_down_scales):
    x = np.asarray(L0_attn_x)
    B, T, C = x.shape
    N = L0_ffn_gate_W_packed.shape[0]
    cfg = Cfg(T=T, C=C, N=N)
    nc = build_program(cfg)
    maps = _in_maps(cfg, x, np.asarray(L0_attn_W), np.asarray(L0_attn_b),
                    np.asarray(L0_ffn_gate_W_packed), np.asarray(L0_ffn_gate_scales),
                    np.asarray(L0_ffn_up_W_packed), np.asarray(L0_ffn_up_scales),
                    np.asarray(L0_ffn_down_W_packed), np.asarray(L0_ffn_down_scales))
    res = run_bass_kernel_spmd(nc, maps, core_ids=list(range(NCORES)))
    return np.stack([res.results[r]["out"] for r in range(NCORES)], axis=0)


if __name__ == "__main__":
    pass
